# version 34
# baseline (speedup 1.0000x reference)
import numpy as np
import ml_dtypes
from contextlib import ExitStack

import concourse.bass as bass
import concourse.mybir as mybir
from concourse.bass_utils import run_bass_kernel_spmd

F32 = mybir.dt.float32
BF16 = mybir.dt.bfloat16
I32 = mybir.dt.int32
AF = mybir.ActivationFunctionType
ALU = mybir.AluOpType
AX = mybir.AxisListType

B, T, D = 4, 4096, 2048
NH, NKV, HD = 16, 4, 128
NREP = NH // NKV
KVD = NKV * HD
NSLOT = 16
C4 = float(128.0 ** 0.25)
SHIFT = 52.0
NEG = -1e9
EPS = 1e-5

ENG2BLOCK = {"sp": "sync", "act": "scalar", "dve": "vector",
             "pe": "tensor", "pool": "gpsimd"}


class Sch:

    NDMASEM = 8

    def __init__(self, nc, es):
        self.nc = nc
        self.sem = {k: es.enter_context(nc.semaphore("sem_" + k)) for k in ENG2BLOCK}
        self.cnt = {k: 0 for k in ENG2BLOCK}
        for q in ("sp", "pool", "act"):
            for i in range(Sch.NDMASEM):
                k = f"{q}~{i}"
                self.sem[k] = es.enter_context(nc.semaphore("sem_" + k.replace("~", "_")))
                self.cnt[k] = 0
        self.dmarot = {q: 0 for q in ("sp", "pool", "act")}
        self.steps = {k: [] for k in ENG2BLOCK}
        self.last_write = {}
        self.readers = {}
        self.pending = {k: [] for k in ENG2BLOCK}

    def emit(self, eng, fn, reads=(), writes=(), deps=(), inc=1, inc_sem=None):
        d = [t for t in deps if t is not None]
        if getattr(self, "_pbar", None) and self._pfirst.get(eng):
            d.extend(self._pbar)
            self._pfirst[eng] = False
        for k in reads:
            w = self.last_write.get(k)
            if w is not None:
                d.append(w)
        for k in writes:
            w = self.last_write.get(k)
            if w is not None:
                d.append(w)
            d.extend(t for t in self.readers.get(k, []) if t is not None)
        sem_key = inc_sem or eng
        self.steps[eng].append((d, fn, inc, sem_key))
        if inc:
            self.cnt[sem_key] += inc
            tok = (sem_key, self.cnt[sem_key])
            allr, allw = list(reads), list(writes)
            for (pr, pw) in self.pending[eng]:
                allr += pr
                allw += pw
            self.pending[eng] = []
            for k in allr:
                self.readers.setdefault(k, []).append(tok)
            for k in allw:
                self.last_write[k] = tok
                self.readers[k] = []
            return tok
        else:
            self.pending[eng].append((list(reads), list(writes)))
            return None

    def dma(self, eng, out, in_, reads=(), writes=(), deps=(), **kw):
        slot = self.dmarot[eng]
        self.dmarot[eng] = (slot + 1) % Sch.NDMASEM
        sub = f"{eng}~{slot}"
        d = list(deps) + [(sub, self.cnt[sub])]
        return self.emit(eng, lambda e, o=out, i=in_, k=kw: e.dma_start(out=o, in_=i, **k),
                         reads=reads, writes=writes, deps=d, inc=16, inc_sem=sub)

    def barrier(self):
        return [(k, c) for k, c in self.cnt.items() if c > 0]

    def phase(self):
        self._pbar = self.barrier()
        self._pfirst = {k: True for k in ENG2BLOCK}

    def finalize(self, block, final_deps):
        sch = self

        def emitter(name):
            def go(eng_obj):
                hwm = {}
                for deps, fn, inc, sem_key in sch.steps[name]:
                    byeng = {}
                    for (de, dc) in deps:
                        byeng[de] = max(byeng.get(de, 0), dc)
                    for de, dc in sorted(byeng.items()):
                        if dc > 0 and dc > hwm.get(de, 0):
                            eng_obj.wait_ge(sch.sem[de], dc)
                            hwm[de] = dc
                    instr = fn(eng_obj)
                    if inc:
                        instr.then_inc(sch.sem[sem_key], inc)
                if name == "sp":
                    byeng = {}
                    for (de, dc) in final_deps:
                        byeng[de] = max(byeng.get(de, 0), dc)
                    for de, dc in sorted(byeng.items()):
                        eng_obj.wait_ge(sch.sem[de], dc)
            return go

        block.sync(emitter("sp"))
        block.scalar(emitter("act"))
        block.vector(emitter("dve"))
        block.tensor(emitter("pe"))
        block.gpsimd(emitter("pool"))


def _classes():
    return [((s // 2) + 1, s % 2) for s in range(NSLOT)]


def _rows_for_par(par):
    return [4 * (v - 1) + 2 * j + par for (v, j) in _classes()]


def _rope_tables(positions):
    pos = np.asarray(positions, dtype=np.float64)
    inv = 1.0 / (1e6 ** (np.arange(0, HD, 2, dtype=np.float64) / HD))
    ang = pos[None, :] * inv[:, None]
    cs = np.concatenate([np.cos(ang)] * 2, axis=0) / C4
    ss = np.concatenate([np.sin(ang)] * 2, axis=0) / C4
    return cs.astype(np.float32), ss.astype(np.float32)


def _masks_for_par(par):
    out = np.zeros((2, 128, 512), dtype=np.float32)
    for j in range(2):
        pos = 2 * j + par
        m = np.zeros((128, 512), dtype=np.float32)
        for t in range(4):
            if t == pos:
                m[:, 128 * t:128 * (t + 1)] = np.where(
                    np.arange(128)[None, :] <= np.arange(128)[:, None], 0.0, NEG)
            elif t > pos:
                m[:, 128 * t:128 * (t + 1)] = NEG
        out[j] = m
    return out


def build_program():
    nc = bass.Bass("TRN2", target_bir_lowering=False, debug=False)

    def ap(name, shape, dt=F32, kind="ExternalInput"):
        return nc.dram_tensor(name, shape, dt, kind=kind).ap()

    i_xT = ap("xT", [D, T], BF16)
    i_xqT = ap("xqT", [D, 2048], BF16)
    i_xres = ap("xres", [2048, D])
    i_wq = ap("wq", [D, D], BF16)
    i_wk = ap("wk", [D, KVD], BF16)
    i_wv = ap("wv", [D, KVD], BF16)
    i_wo = ap("wo", [D, D], BF16)
    i_bq = ap("bq", [128, NH])
    i_bk = ap("bk", [128, NKV])
    i_bvr = ap("bvrow", [1, KVD])
    i_bor = ap("borow", [1, D])
    i_csq = ap("csq", [HD, 2048])
    i_ssq = ap("ssq", [HD, 2048])
    i_csk = ap("csk", [HD, T])
    i_ssk = ap("ssk", [HD, T])
    i_rt = ap("rt", [HD, HD])
    i_ident = ap("ident", [128, 128])
    i_ones_r = ap("ones_r", [1, 128])
    i_mhalf_c = ap("mhalf_c", [128, 1])
    i_half_r = ap("half_r", [1, 128])
    i_ones_c = ap("ones_c", [128, 1])
    i_masks = ap("masks", [2, 128, 512])
    i_pwT = ap("pwT", [4, D, D])
    i_nwT = ap("nwT", [4, D, D])
    i_pb = ap("pb", [128, 64])
    i_nb = ap("nb", [128, 64])
    i_bexps = ap("bexps", [128, 4])
    i_exps = ap("exps", [128, 4])
    i_maskw = ap("maskw", [128, 4])
    i_scale = ap("scale", [128, 1])
    i_bscale = ap("bscale", [128, 1])

    o_out = ap("out", [2048, D], F32, kind="ExternalOutput")

    s_w = nc.dram_tensor("s_w", [D, D], BF16, kind="Internal").ap()
    s_qt = nc.dram_tensor("s_qt", [NH, HD, 2048], BF16, kind="Internal").ap()
    s_at = nc.dram_tensor("s_at", [NH, HD, 2048], BF16, kind="Internal").ap()
    s_x2 = nc.dram_tensor("s_x2", [2048, D], F32, kind="Internal").ap()
    s_x2b = nc.dram_tensor("s_x2b", [2048, D], BF16, kind="Internal").ap()
    s_fb = nc.dram_tensor("s_fb", [128, 16], F32, kind="Internal").ap()

    es = ExitStack()
    S_ = es.enter_context
    sch = Sch(nc, es)
    E, DMA = sch.emit, sch.dma

    semnums = sorted(h.num for h in sch.sem.values())
    nc.gpsimd.dma_reset(range(semnums[0], semnums[-1] + 1))
    for h in sch.sem.values():
        nc.gpsimd.sem_clear(h)
    nc.all_engine_barrier()

    sb_recip = S_(nc.sbuf_tensor("sb_recip", [128, NSLOT], F32))
    sb_cstk = S_(nc.sbuf_tensor("sb_cstk", [128, NKV], F32))
    sb_id = S_(nc.sbuf_tensor("sb_id", [128, 128], BF16))
    sb_rt = S_(nc.sbuf_tensor("sb_rt", [128, 128], BF16))
    sb_ones_r = S_(nc.sbuf_tensor("sb_ones_r", [1, 128], BF16))
    sb_half_r = S_(nc.sbuf_tensor("sb_half_r", [1, 128], F32))
    sb_mhalf_c = S_(nc.sbuf_tensor("sb_mhalf_c", [128, 1], BF16))
    sb_ones_c = S_(nc.sbuf_tensor("sb_ones_c", [128, 1], BF16))
    sb_bq = S_(nc.sbuf_tensor("sb_bq", [128, NH], F32))
    sb_bk = S_(nc.sbuf_tensor("sb_bk", [128, NKV], F32))
    sb_bvr = S_(nc.sbuf_tensor("sb_bvr", [1, KVD], BF16))
    sb_fbr = S_(nc.sbuf_tensor("sb_fbr", [1, D], BF16))
    sb_em = S_(nc.sbuf_tensor("sb_em", [128, 4], F32))
    sb_emn = S_(nc.sbuf_tensor("sb_emn", [128, 4], F32))
    sb_sm = S_(nc.sbuf_tensor("sb_sm", [128, 64], F32))
    sb_smB = S_(nc.sbuf_tensor("sb_smB", [128, 192], F32))
    sb_acc = S_(nc.sbuf_tensor("sb_acc", [128, 8], F32))
    sb_i32 = S_(nc.sbuf_tensor("sb_i32", [128, 16], I32))

    ps_a = S_(nc.psum_tensor("ps_a", [128, 512], F32))
    ps_b = S_(nc.psum_tensor("ps_b", [128, 512], F32))
    ps_c = S_(nc.psum_tensor("ps_c", [128, 512], F32))
    ps_d = S_(nc.psum_tensor("ps_d", [128, 512], F32))
    ps_e = S_(nc.psum_tensor("ps_e", [128, 512], F32))
    ps_f = S_(nc.psum_tensor("ps_f", [128, 256], F32))
    ps_s = S_(nc.psum_tensor("ps_s", [128, 16], F32))
    ps_r1 = S_(nc.psum_tensor("ps_r1", [1, 512], F32))

    block = S_(nc.Block())

    DMA("pool", sb_id[:], i_ident[:, :], writes=["id"])
    DMA("pool", sb_rt[:], i_rt[:, :], writes=["rt"])
    DMA("pool", sb_ones_r[:], i_ones_r[:, :], writes=["ones_r"])
    DMA("pool", sb_half_r[:], i_half_r[:, :], writes=["half_r"])
    DMA("pool", sb_mhalf_c[:], i_mhalf_c[:, :], writes=["mhalf"])
    DMA("pool", sb_ones_c[:], i_ones_c[:, :], writes=["ones_c"])
    DMA("sp", sb_bq[:], i_bq[:, :], writes=["bq"])
    DMA("sp", sb_bk[:], i_bk[:, :], writes=["bk"])
    DMA("pool", sb_bvr[:], i_bvr[:, :], writes=["bvr"])
    DMA("sp", sb_smB[:, 0:4], i_exps[:, :], writes=["smB"])
    DMA("sp", sb_smB[:, 4:8], i_maskw[:, :], writes=["smB"])
    DMA("sp", sb_smB[:, 8:9], i_scale[:, :], writes=["smB"])
    DMA("sp", sb_smB[:, 9:10], i_bscale[:, :], writes=["smB"])
    DMA("sp", sb_smB[:, 16:80], i_pb[:, :], writes=["smB"])
    DMA("sp", sb_smB[:, 80:144], i_nb[:, :], writes=["smB"])
    DMA("sp", sb_smB[:, 144:148], i_bexps[:, :], writes=["smB"])
    es_kv = es.enter_context(ExitStack())
    sb_kt = es_kv.enter_context(nc.sbuf_tensor("sb_kt", [128, NKV * T], BF16))
    sb_v = es_kv.enter_context(nc.sbuf_tensor("sb_v", [128, 32 * KVD], BF16))
    sb_negc = es_kv.enter_context(nc.sbuf_tensor("sb_negc", [128, NSLOT * NH], F32))
    sb_masks = es_kv.enter_context(nc.sbuf_tensor("sb_masks", [128, 1024], BF16))
    DMA("pool", sb_masks[:].rearrange("p (j f) -> p j f", j=2),
        i_masks[:, :, :].rearrange("j p f -> p j f"), writes=["masks"])
    es_pr = es.enter_context(ExitStack())
    pA = [es_pr.enter_context(nc.sbuf_tensor(f"prA{i}", [128, 4096], BF16)) for i in range(2)]
    pT = [es_pr.enter_context(nc.sbuf_tensor(f"prT{i}", [128, 2048], F32)) for i in range(2)]
    pR = [es_pr.enter_context(nc.sbuf_tensor(f"prR{i}", [128, 2048], F32)) for i in range(2)]
    pK = es_pr.enter_context(nc.sbuf_tensor("prK", [128, 4096], BF16))
    sb_row = es_pr.enter_context(nc.sbuf_tensor("sb_row", [1, 512], F32))
    pSqk = es_pr.enter_context(nc.sbuf_tensor("prSqk", [128, 1024], BF16))

    def _phaseA():
        with ExitStack() as ph:
            P = ph.enter_context
            pWk = [P(nc.sbuf_tensor(f"aWk{i}", [128, 2048], BF16)) for i in range(4)]
            pWv = [P(nc.sbuf_tensor(f"aWv{i}", [128, 2048], BF16)) for i in range(4)]
            pPl = [P(nc.sbuf_tensor(f"aPl{i}", [128, 1024], F32)) for i in range(4)]
            pS = [P(nc.sbuf_tensor(f"aS{i}", [128, 2048], BF16)) for i in range(2)]
            pAcc = P(nc.sbuf_tensor("aAcc", [128, 2048], F32))
            pOut = P(nc.sbuf_tensor("aOut", [128, 2048], BF16))

            E("act", lambda e: e.activation(out=sb_smB[:, 12:16], in_=sb_smB[:, 4:8],
                                            func=AF.Sigmoid), reads=["smB"], writes=["smB"])
            E("dve", lambda e: e.tensor_tensor(out=sb_em[:], in0=sb_smB[:, 0:4],
                                               in1=sb_smB[:, 12:16], op=ALU.mult),
              reads=["smB"], writes=["em"])
            E("dve", lambda e: e.tensor_scalar(out=sb_em[:], in0=sb_em[:],
                                               scalar1=sb_smB[:, 8:9], scalar2=None,
                                               op0=ALU.mult), reads=["smB", "em"], writes=["em"])
            E("dve", lambda e: e.tensor_scalar(out=sb_emn[:], in0=sb_em[:],
                                               scalar1=-1.0, scalar2=None,
                                               op0=ALU.mult), reads=["em"], writes=["emn"])
            E("dve", lambda e: e.tensor_tensor(out=sb_smB[:, 16:80], in0=sb_smB[:, 16:80],
                                               in1=sb_smB[:, 80:144], op=ALU.subtract),
              reads=["smB"], writes=["smB"])
            for n in range(4):
                if n == 0:
                    E("dve", lambda e, n=n: e.tensor_scalar(
                        out=sb_smB[:, 160:176], in0=sb_smB[:, 16 + 16 * n:32 + 16 * n],
                        scalar1=sb_smB[:, 144 + n:145 + n], scalar2=None, op0=ALU.mult),
                        reads=["smB"], writes=["smB"])
                else:
                    E("dve", lambda e, n=n: e.scalar_tensor_tensor(
                        out=sb_smB[:, 160:176], in0=sb_smB[:, 16 + 16 * n:32 + 16 * n],
                        scalar=sb_smB[:, 144 + n:145 + n], in1=sb_smB[:, 160:176],
                        op0=ALU.mult, op1=ALU.add), reads=["smB"], writes=["smB"])
            E("dve", lambda e: e.tensor_scalar(out=sb_smB[:, 160:176], in0=sb_smB[:, 160:176],
                                               scalar1=1.0, scalar2=-1.0,
                                               op0=ALU.min, op1=ALU.max),
              reads=["smB"], writes=["smB"])
            E("dve", lambda e: e.tensor_scalar(out=sb_smB[:, 160:176], in0=sb_smB[:, 160:176],
                                               scalar1=15.0, scalar2=None, op0=ALU.mult),
              reads=["smB"], writes=["smB"])
            E("dve", lambda e: e.tensor_copy(out=sb_i32[:], in_=sb_smB[:, 160:176]),
              reads=["smB"], writes=["i32"])
            E("dve", lambda e: e.tensor_copy(out=sb_smB[:, 160:176], in_=sb_i32[:]),
              reads=["i32"], writes=["smB"])
            E("dve", lambda e: e.tensor_scalar(out=sb_smB[:, 160:176], in0=sb_smB[:, 160:176],
                                               scalar1=sb_smB[:, 9:10], scalar2=1.0 / 15.0,
                                               op0=ALU.mult, op1=ALU.mult),
              reads=["smB"], writes=["smB"])
            DMA("sp", s_fb[:, :], sb_smB[:, 160:176], reads=["smB"], writes=["s_fb"])
            DMA("pool", sb_fbr[:, :], s_fb[:, :].rearrange("p c -> (p c)")[None, :],
                reads=["s_fb"], writes=["fbr"])

            QS = ("sp", "pool", "act")
            for dc in range(16):
                DMA(QS[dc % 3], pWk[dc // 4][:, 512 * (dc % 4):512 * (dc % 4 + 1)],
                    i_wk[128 * dc:128 * (dc + 1), :], writes=["wk"])
                DMA(QS[(dc + 2) % 3], pWv[dc // 4][:, 512 * (dc % 4):512 * (dc % 4 + 1)],
                    i_wv[128 * dc:128 * (dc + 1), :], writes=["wv"])

            def p1_chunk(oc):
                for n in range(4):
                  for h in range(2):
                    st = 2 * n + h
                    pw_b, nw_b = 2 * (st % 2), 2 * (st % 2) + 1
                    cs = slice(1024 * h, 1024 * (h + 1))
                    Q3 = ("sp", "pool", "act")
                    DMA(Q3[(2 * st) % 3], pPl[pw_b][:, :],
                        i_pwT[n, 128 * oc:128 * (oc + 1), 1024 * h:1024 * (h + 1)],
                        writes=[f"aPl{pw_b}"])
                    DMA(Q3[(2 * st + 1) % 3], pPl[nw_b][:, :],
                        i_nwT[n, 128 * oc:128 * (oc + 1), 1024 * h:1024 * (h + 1)],
                        writes=[f"aPl{nw_b}"])
                    E("act", lambda e, pw_b=pw_b, cs=cs: e.activation(
                        out=pS[0][:, cs], in_=pPl[pw_b][:, :], func=AF.Sigmoid),
                      reads=[f"aPl{pw_b}"], writes=[f"aS0.{h}"])
                    E("act", lambda e, nw_b=nw_b, cs=cs: e.activation(
                        out=pS[1][:, cs], in_=pPl[nw_b][:, :], func=AF.Sigmoid),
                      reads=[f"aPl{nw_b}"], writes=[f"aS1.{h}"])
                    if n == 0:
                        E("dve", lambda e, n=n, cs=cs: e.tensor_scalar(
                            out=pAcc[:, cs], in0=pS[0][:, cs], scalar1=sb_em[:, n:n + 1],
                            scalar2=None, op0=ALU.mult),
                            reads=[f"aS0.{h}", "em"], writes=[f"aAcc.{h}"])
                    else:
                        E("dve", lambda e, n=n, cs=cs: e.scalar_tensor_tensor(
                            out=pAcc[:, cs], in0=pS[0][:, cs], scalar=sb_em[:, n:n + 1],
                            in1=pAcc[:, cs], op0=ALU.mult, op1=ALU.add),
                            reads=[f"aS0.{h}", "em", f"aAcc.{h}"], writes=[f"aAcc.{h}"])
                    E("dve", lambda e, n=n, cs=cs: e.scalar_tensor_tensor(
                        out=pAcc[:, cs], in0=pS[1][:, cs], scalar=sb_emn[:, n:n + 1],
                        in1=pAcc[:, cs], op0=ALU.mult, op1=ALU.add),
                        reads=[f"aS1.{h}", "emn", f"aAcc.{h}"], writes=[f"aAcc.{h}"])
                E("dve", lambda e: e.tensor_copy(out=pOut[:, :], in_=pAcc[:, :]),
                  reads=["aAcc.0", "aAcc.1"], writes=["aOut"])
                DMA("act", s_w[128 * oc:128 * (oc + 1), :], pOut[:, :],
                    reads=["aOut"], writes=["s_w"])

            def rope_block(tc):
                tc8 = tc % 8
                for g in range(NKV):
                    kck = f"prK.{g}.{tc % 2}"
                    awin = 1024 * (tc % 2) + 256 * g
                    bwin = 2048 + 1024 * (tc % 2) + 256 * g
                    rps = ps_c if g % 2 == 0 else ps_f
                    rpsk = "ps_c" if g % 2 == 0 else "ps_f"
                    E("pe", lambda e, rps=rps, awin=awin: e.matmul(
                        out=rps[:, 0:256], lhsT=sb_rt[:],
                        rhs=pK[:, awin:awin + 256], start=True, stop=True),
                        reads=[kck + "a", "rt"], writes=[rpsk])
                    E("act", lambda e, rps=rps, bwin=bwin: e.activation(
                        out=pK[:, bwin:bwin + 256],
                        in_=rps[:, 0:256], func=AF.Copy),
                        reads=[rpsk], writes=[kck + "b"])
                    E("dve", lambda e, g=g, tc8=tc8, awin=awin: e.tensor_tensor(
                        out=pR[0][:, 512 * (g % 2):512 * (g % 2) + 256],
                        in0=pK[:, awin:awin + 256],
                        in1=pT[0][:, 256 * tc8:256 * (tc8 + 1)], op=ALU.mult),
                        reads=[kck + "a", "prT0"], writes=[f"prR0.{g % 2}"])
                    E("pool", lambda e, g=g, tc8=tc8, bwin=bwin: e.tensor_tensor(
                        out=pR[1][:, 512 * (g % 2):512 * (g % 2) + 256],
                        in0=pK[:, bwin:bwin + 256],
                        in1=pT[1][:, 256 * tc8:256 * (tc8 + 1)], op=ALU.mult),
                        reads=[kck + "b", "prT1"], writes=[f"prR1.{g % 2}"])
                    E("dve", lambda e, g=g, tc=tc: e.tensor_tensor(
                        out=sb_kt[:, T * g + 256 * tc: T * g + 256 * (tc + 1)],
                        in0=pR[0][:, 512 * (g % 2):512 * (g % 2) + 256],
                        in1=pR[1][:, 512 * (g % 2):512 * (g % 2) + 256], op=ALU.add),
                        reads=[f"prR0.{g % 2}", f"prR1.{g % 2}"], writes=[f"kt{g}"])
                    E("dve", lambda e, g=g, tc=tc: e.tensor_tensor(
                        out=pSqk[:, 512 * (tc % 2) + 256 * (g % 2):
                                 512 * (tc % 2) + 256 * (g % 2) + 256],
                        in0=sb_kt[:, T * g + 256 * tc: T * g + 256 * (tc + 1)],
                        in1=sb_kt[:, T * g + 256 * tc: T * g + 256 * (tc + 1)],
                        op=ALU.mult),
                        reads=[f"kt{g}"], writes=[f"prSqk{g}.{tc % 2}"])

            def km_block(tc):
                for g in range(NKV):
                    E("pe", lambda e, g=g, tc=tc: e.matmul(
                        out=ps_r1[0:1, 0:256],
                        lhsT=sb_ones_c[:],
                        rhs=pSqk[:, 512 * (tc % 2) + 256 * (g % 2):
                                 512 * (tc % 2) + 256 * (g % 2) + 256],
                        start=True, stop=True),
                        reads=[f"prSqk{g}.{tc % 2}", "ones_c"], writes=["ps_r1"])
                    E("dve", lambda e, g=g, tc=tc: e.reduce_max(
                        out=sb_row[0:1, 16 * g + tc:16 * g + tc + 1],
                        in_=ps_r1[0:1, 0:256], axis=AX.X),
                        reads=["ps_r1"], writes=["row"])

            def load_chunk(tc):
                if tc >= 16:
                    return
                DMA(("pool", "sp")[tc % 2], pA[tc % 2][:].rearrange("p (c f) -> p c f", c=16),
                    i_xT[:, 256 * tc:256 * (tc + 1)].rearrange("(c p) f -> p c f", p=128),
                    writes=[f"prA{tc % 2}"])

            def p3_chunk(tc):
                half, tc8 = tc // 8, tc % 8
                load_chunk(tc + 1)
                if tc >= 2:
                    km_block(tc - 2)
                if tc >= 1:
                    rope_block(tc - 1)
                if tc8 == 0:
                    DMA("sp", pT[0][:, :], i_csk[:, 2048 * half:2048 * (half + 1)],
                        writes=["prT0"])
                    DMA("sp", pT[1][:, :], i_ssk[:, 2048 * half:2048 * (half + 1)],
                        writes=["prT1"])
                bA = pA[tc % 2]
                key = f"prA{tc % 2}"
                for g in range(NKV):
                    ps = ps_a if g % 2 == 0 else ps_b
                    pskey = "ps_a" if g % 2 == 0 else "ps_b"
                    for dc in range(16):
                        lhs = pWk[dc // 4][:, 512 * (dc % 4) + 128 * g:
                                           512 * (dc % 4) + 128 * (g + 1)]
                        E("pe", lambda e, l=lhs, b=bA, dc=dc, ps=ps: e.matmul(
                            out=ps[:, 0:256], lhsT=l, rhs=b[:, 256 * dc:256 * (dc + 1)],
                            start=(dc == 0), stop=(dc == 15)),
                            reads=["wk", key], writes=[pskey] if dc == 0 else [],
                            inc=(1 if dc == 15 else 0))
                    awin = 1024 * (tc % 2) + 256 * g
                    E("act", lambda e, g=g, ps=ps, awin=awin: e.activation(
                        out=pK[:, awin:awin + 256], in_=ps[:, 0:256],
                        func=AF.Identity, bias=sb_bk[:, g:g + 1], scale=1.0),
                        reads=[pskey, "bk"], writes=[f"prK.{g}.{tc % 2}a"])
                for rt_ in range(2):
                    ktile = 2 * tc + rt_
                    psd = ps_d if rt_ == 0 else ps_e
                    psdk = "ps_d" if rt_ == 0 else "ps_e"
                    E("pe", lambda e, psd=psd: e.matmul(out=psd[:, :], lhsT=sb_ones_r[:],
                                                        rhs=sb_bvr[:, :], start=True, stop=False),
                      reads=["ones_r", "bvr", key], writes=[psdk], inc=0)
                    for dc in range(16):
                        E("pe", lambda e, dc=dc, rt_=rt_, b=bA, psd=psd: e.matmul(
                            out=psd[:, :],
                            lhsT=b[:, 256 * dc + 128 * rt_: 256 * dc + 128 * (rt_ + 1)],
                            rhs=pWv[dc // 4][:, 512 * (dc % 4):512 * (dc % 4 + 1)],
                            start=False, stop=(dc == 15)),
                            reads=["wv", key], writes=[],
                            inc=(1 if dc == 15 else 0))
                    E("act", lambda e, ktile=ktile, psd=psd: e.activation(
                        out=sb_v[:, KVD * ktile:KVD * (ktile + 1)], in_=psd[:, :],
                        func=AF.Copy), reads=[psdk], writes=["v"])

            load_chunk(0)
            for i in range(16):
                p3_chunk(i)
                p1_chunk(i)
            km_block(14)
            rope_block(15)
            km_block(15)

            for g in range(NKV):
                E("dve", lambda e, g=g: e.reduce_max(out=sb_row[0:1, 256 + g:257 + g],
                                                     in_=sb_row[0:1, 16 * g:16 * (g + 1)],
                                                     axis=AX.X), reads=["row"], writes=["row2"])
                E("pe", lambda e, g=g: e.matmul(out=ps_s[:, g:g + 1], lhsT=sb_half_r[:],
                                                rhs=sb_row[0:1, 256 + g:257 + g],
                                                start=True, stop=True),
                  reads=["row2", "half_r"], writes=["ps_s"])
                E("dve", lambda e, g=g: e.tensor_copy(out=sb_cstk[:, g:g + 1],
                                                      in_=ps_s[:, g:g + 1]),
                  reads=["ps_s"], writes=["cstk"])
    _phaseA()
    sch.phase()

    def _phaseB():
        with ExitStack() as ph:
            P = ph.enter_context
            pQr = [P(nc.sbuf_tensor(f"bQr{i}", [128, 4096], BF16)) for i in range(2)]
            pQ = P(nc.sbuf_tensor("bQ", [128, 2048], BF16))
            pRk = P(nc.sbuf_tensor("bRk", [128, 2048], BF16))
            pSq = P(nc.sbuf_tensor("bSq", [128, 2048], BF16))
            pWq = [P(nc.sbuf_tensor(f"bWq{i}", [128, 2048], BF16)) for i in range(8)]

            DMA("sp", pT[0][:, :], i_csq[:, :], writes=["prT0"])
            DMA("sp", pT[1][:, :], i_ssq[:, :], writes=["prT1"])

            def load_wq(quarter):
                if quarter >= 4:
                    return
                o = 4 * (quarter % 2)
                for dc in range(16):
                    DMA("act",
                        pWq[o + dc // 4][:, 512 * (dc % 4):512 * (dc % 4 + 1)],
                        i_wq[128 * dc:128 * (dc + 1),
                             512 * quarter:512 * (quarter + 1)],
                        writes=[f"wq{quarter % 2}"])

            load_wq(0)
            for quarter in range(4):
                load_wq(quarter + 1)
                wqo, wqkey = 4 * (quarter % 2), f"wq{quarter % 2}"
                def load_q(tc):
                    if tc >= 8:
                        return
                    DMA(("pool", "sp")[tc % 2],
                        pA[tc % 2][:].rearrange("p (c f) -> p c f", c=16),
                        i_xqT[:, 256 * tc:256 * (tc + 1)].rearrange("(c p) f -> p c f",
                                                                   p=128),
                        writes=[f"prA{tc % 2}"])
                load_q(0)
                for tc in range(8):
                    load_q(tc + 1)
                    bA = pA[tc % 2]
                    key = f"prA{tc % 2}"
                    for hh in range(4):
                        h = 4 * quarter + hh
                        ps = ps_a if hh % 2 == 0 else ps_b
                        pskey = "ps_a" if hh % 2 == 0 else "ps_b"
                        for dc in range(16):
                            lhs = pWq[wqo + dc // 4][:, 512 * (dc % 4) + 128 * hh:
                                               512 * (dc % 4) + 128 * (hh + 1)]
                            E("pe", lambda e, l=lhs, b=bA, dc=dc, ps=ps: e.matmul(
                                out=ps[:, 0:256], lhsT=l, rhs=b[:, 256 * dc:256 * (dc + 1)],
                                start=(dc == 0), stop=(dc == 15)),
                                reads=[wqkey, key], writes=[pskey] if dc == 0 else [],
                                inc=(1 if dc == 15 else 0))
                        dst = pQr[hh // 2][:, 2048 * (hh % 2) + 256 * tc:
                                           2048 * (hh % 2) + 256 * (tc + 1)]
                        E("act", lambda e, h=h, dst=dst, ps=ps: e.activation(
                            out=dst, in_=ps[:, 0:256], func=AF.Identity,
                            bias=sb_bq[:, h:h + 1], scale=1.0),
                            reads=[pskey, "bq"], writes=[f"qraw{hh}"])
                for hh in range(4):
                    h = 4 * quarter + hh
                    qraw = pQr[hh // 2][:, 2048 * (hh % 2):2048 * (hh % 2 + 1)]
                    for cc in range(4):
                        E("pe", lambda e, qraw=qraw, cc=cc: e.matmul(
                            out=(ps_c if cc % 2 == 0 else ps_d)[:, :], lhsT=sb_rt[:],
                            rhs=qraw[:, 512 * cc:512 * (cc + 1)], start=True, stop=True),
                            reads=[f"qraw{hh}", "rt"],
                            writes=["ps_c" if cc % 2 == 0 else "ps_d"])
                        E("act", lambda e, cc=cc: e.activation(
                            out=pRk[:, 512 * cc:512 * (cc + 1)],
                            in_=(ps_c if cc % 2 == 0 else ps_d)[:, :], func=AF.Copy),
                            reads=["ps_c" if cc % 2 == 0 else "ps_d"], writes=["bRk"])
                    E("dve", lambda e, qraw=qraw: e.tensor_tensor(
                        out=pR[0][:, :], in0=qraw, in1=pT[0][:, :], op=ALU.mult),
                        reads=[f"qraw{hh}", "prT0"], writes=["prR0"])
                    E("pool", lambda e: e.tensor_tensor(
                        out=pR[1][:, :], in0=pRk[:, :], in1=pT[1][:, :], op=ALU.mult),
                        reads=["bRk", "prT1"], writes=["prR1"])
                    E("dve", lambda e: e.tensor_tensor(out=pQ[:, :], in0=pR[0][:, :],
                                                       in1=pR[1][:, :], op=ALU.add),
                      reads=["prR0", "prR1"], writes=["bQ"])
                    DMA("sp", s_qt[h, :, :], pQ[:, :], reads=["bQ"], writes=[f"s_qt{h}"])
                    E("dve", lambda e: e.tensor_tensor(out=pSq[:, :], in0=pQ[:, :],
                                                       in1=pQ[:, :], op=ALU.mult),
                      reads=["bQ"], writes=["bSq"])
                    nrot = [ps_s, ps_e, ps_f]
                    nrotk = ["ps_s", "ps_e", "ps_f"]
                    for s in range(NSLOT):
                        nps = nrot[s % 3]
                        npsk = nrotk[s % 3]
                        E("pe", lambda e, s=s, nps=nps: e.matmul(
                            out=nps[:, 0:1],
                            lhsT=pSq[:, 128 * s:128 * (s + 1)], rhs=sb_mhalf_c[:],
                            start=True, stop=True),
                            reads=["bSq", "mhalf"], writes=[npsk])
                        g = h // NREP
                        E("dve", lambda e, s=s, h=h, g=g, nps=nps: e.scalar_tensor_tensor(
                            out=sb_negc[:, NH * s + h:NH * s + h + 1],
                            in0=nps[:, 0:1], scalar=SHIFT,
                            in1=sb_cstk[:, g:g + 1], op0=ALU.add, op1=ALU.subtract),
                            reads=[npsk, "cstk"], writes=["negc"])
    _phaseB()
    es_pr.close()
    sch.phase()

    def _phaseC():
        with ExitStack() as ph:
            P = ph.enter_context
            pPT = [P(nc.sbuf_tensor(f"cPT{i}", [128, 8192], BF16, is_transpose_dest=True))
                   for i in range(4)]
            pP = [P(nc.sbuf_tensor(f"cP{i}", [128, 4096], BF16)) for i in range(6)]
            pQl = [P(nc.sbuf_tensor(f"cQ{i}", [128, 256], BF16)) for i in range(4)]
            pAT = [P(nc.sbuf_tensor(f"cAT{i}", [128, 256], BF16)) for i in range(4)]

            psrot = [ps_a, ps_b, ps_e]
            psrotk = ["ps_a", "ps_b", "ps_e"]
            units = [(v, h) for v in range(1, 9) for h in range(NH)]
            NU = len(units)
            sjc = 0

            def emit_qld(u):
                v, h = units[u]
                ub = u % 4
                DMA("pool", pQl[ub][:, :], s_qt[h, :, 256 * (v - 1):256 * v],
                    reads=[f"s_qt{h}"], writes=[f"cQ{ub}"])

            def emit_front(u):
                nonlocal sjc
                v, h = units[u]
                g = h // NREP
                ub = u % 4
                for sl in range(2):
                    s = 2 * (v - 1) + sl
                    pbuf = pP[2 * (u % 3) + sl]
                    pkey = f"cP{2 * (u % 3) + sl}"
                    for sj in range(v):
                        ps = psrot[sjc % 3]
                        pskey = psrotk[sjc % 3]
                        sjc += 1
                        E("pe", lambda e, sl=sl, sj=sj, ps=ps, g=g, ub=ub: e.matmul(
                            out=ps[:, :], lhsT=pQl[ub][:, 128 * sl:128 * (sl + 1)],
                            rhs=sb_kt[:, T * g + 512 * sj: T * g + 512 * (sj + 1)],
                            start=True, stop=(sj != v - 1)),
                            reads=[f"cQ{ub}", f"kt{g}"], writes=[pskey],
                            inc=(1 if sj != v - 1 else 0))
                        if sj == v - 1:
                            E("pe", lambda e, sl=sl, ps=ps: e.matmul(
                                out=ps[:, :], lhsT=sb_id[:],
                                rhs=sb_masks[:, 512 * sl:512 * (sl + 1)],
                                start=False, stop=True),
                                reads=["masks", "id"], writes=[])
                        E("act", lambda e, s=s, h=h, sj=sj, pbuf=pbuf, ps=ps: e.activation(
                            out=pbuf[:, 512 * sj:512 * (sj + 1)], in_=ps[:, :],
                            func=AF.Exp,
                            bias=sb_negc[:, NH * s + h:NH * s + h + 1], scale=1.0,
                            accum_out=sb_acc[:, sj:sj + 1]),
                            reads=[pskey, "negc"], writes=[pkey, f"acc{sj}"])
                    fence = E("act", lambda e: e.copy(out=sb_sm[0:1, 62:63],
                                                      in_=sb_sm[0:1, 61:62]))
                    E("dve", lambda e, sl=sl, v=v: e.reduce_sum(
                        out=sb_recip[:, sl:sl + 1], in_=sb_acc[:, 0:v], axis=AX.X),
                        reads=[f"acc{sj}" for sj in range(v)], writes=[f"rtmp{sl}"],
                        deps=[fence])
                    E("dve", lambda e, sl=sl: e.reciprocal(
                        out=sb_recip[:, sl:sl + 1], in_=sb_recip[:, sl:sl + 1]),
                        reads=[f"rtmp{sl}"], writes=[f"rtmp{sl}"])
                    E("dve", lambda e, sl=sl, v=v, pbuf=pbuf: e.tensor_scalar(
                        out=pbuf[:, 0:512 * v], in0=pbuf[:, 0:512 * v],
                        scalar1=sb_recip[:, sl:sl + 1], scalar2=None, op0=ALU.mult),
                        reads=[pkey, f"rtmp{sl}"], writes=[pkey])
                    DMA(("sp", "act")[sl],
                        pPT[ub][:].rearrange("p (j q f) -> p j q f", j=32, q=2)
                        [:, 0:4 * v, sl, :],
                        pbuf[:, 0:512 * v], transpose=True,
                        reads=[pkey], writes=[f"cPT{ub}"])

            def emit_back(u):
                v, h = units[u]
                g = h // NREP
                ub = u % 4
                pcd = ps_c if u % 2 == 0 else ps_d
                pcdk = "ps_c" if u % 2 == 0 else "ps_d"
                for j in range(4 * v):
                    E("pe", lambda e, j=j, g=g, ub=ub, pcd=pcd: e.matmul(
                        out=pcd[:, 0:256],
                        lhsT=sb_v[:, KVD * j + 128 * g: KVD * j + 128 * (g + 1)],
                        rhs=pPT[ub][:, 256 * j:256 * (j + 1)],
                        start=(j == 0), stop=(j == 4 * v - 1)),
                        reads=["v", f"cPT{ub}"],
                        writes=[pcdk] if j == 0 else [],
                        inc=(1 if j == 4 * v - 1 else 0))
                E("dve", lambda e, pcd=pcd, ub=ub: e.tensor_copy(
                    out=pAT[ub][:, :], in_=pcd[:, 0:256]),
                    reads=[pcdk], writes=[f"cAT{ub}"])
                DMA("pool", s_at[h, :, 256 * (v - 1):256 * v], pAT[ub][:, :],
                    reads=[f"cAT{ub}"], writes=[f"s_at{h}"])

            emit_qld(0)
            emit_qld(1)
            emit_qld(2)
            for u in range(NU):
                if u + 3 < NU:
                    emit_qld(u + 3)
                emit_front(u)
                if u >= 1:
                    emit_back(u - 1)
            emit_back(NU - 1)
        es_kv.close()
    _phaseC()
    sch.phase()

    def _phaseD():
        with ExitStack() as ph:
            P = ph.enter_context
            pW = [P(nc.sbuf_tensor(f"dW{i}", [128, 2048], BF16)) for i in range(16)]
            pW6 = [P(nc.sbuf_tensor(f"dW6{i}", [128, 2048], BF16)) for i in range(16)]
            pAtt = [P(nc.sbuf_tensor(f"dA{i}", [128, 2048], BF16)) for i in range(2)]
            pH = [P(nc.sbuf_tensor(f"dH{i}", [128, 2048], F32)) for i in range(2)]
            pH6 = [P(nc.sbuf_tensor(f"dG{i}", [128, 2048], F32)) for i in range(2)]
            pX = [P(nc.sbuf_tensor(f"dX{i}", [128, 2048], F32)) for i in range(2)]
            pXT = [P(nc.sbuf_tensor(f"dT{i}", [128, 2048], BF16, is_transpose_dest=True))
                   for i in range(2)]
            pB16 = P(nc.sbuf_tensor("dB0", [128, 2048], BF16))
            sb_bor = P(nc.sbuf_tensor("sb_bor", [1, D], BF16))

            DMA("pool", sb_bor[:, :], i_bor[:, :], writes=["bor"])
            for dc in range(16):
                DMA("pool", pW[dc][:, :],
                    i_wo[128 * dc:128 * (dc + 1), :], writes=["wo"])
                DMA(("sp", "act")[dc % 2], pW6[dc][:, :],
                    s_w[:, 128 * dc:128 * (dc + 1)], transpose=True,
                    reads=["s_w"], writes=["wt"])
            psrot = [ps_a, ps_b, ps_e]
            psrotk = ["ps_a", "ps_b", "ps_e"]
            grp = [0]

            def p5_mm(s5):
                bf = s5 % 2
                DMA("act", pAtt[bf][:].rearrange("p (c f) -> p c f", f=128),
                    s_at[:, :, 128 * s5:128 * (s5 + 1)].rearrange("c p f -> p c f"),
                    reads=[f"s_at{h}" for h in range(NH)], writes=[f"dA{bf}"])
                DMA("pool", pX[bf][:, :], i_xres[128 * s5:128 * (s5 + 1), :],
                    writes=[f"dX{bf}"])
                for ocq in range(4):
                    ps = psrot[grp[0] % 3]
                    pskey = psrotk[grp[0] % 3]
                    grp[0] += 1
                    E("pe", lambda e, ocq=ocq, ps=ps: e.matmul(
                        out=ps[:, :], lhsT=sb_ones_r[:],
                        rhs=sb_bor[0:1, 512 * ocq:512 * (ocq + 1)], start=True, stop=False),
                        reads=["bor", "ones_r"], writes=[pskey], inc=0)
                    for cch in range(16):
                        E("pe", lambda e, cch=cch, ocq=ocq, bf=bf, ps=ps: e.matmul(
                            out=ps[:, :], lhsT=pAtt[bf][:, 128 * cch:128 * (cch + 1)],
                            rhs=pW[cch][:, 512 * ocq:512 * (ocq + 1)],
                            start=False, stop=(cch == 15)),
                            reads=[f"dA{bf}", "wo"], writes=[],
                            inc=(1 if cch == 15 else 0))
                    E("dve", lambda e, ocq=ocq, bf=bf, ps=ps: e.tensor_tensor(
                        out=pH[bf][:, 512 * ocq:512 * (ocq + 1)], in0=ps[:, :],
                        in1=pX[bf][:, 512 * ocq:512 * (ocq + 1)], op=ALU.add),
                        reads=[pskey, f"dX{bf}"], writes=[f"dH{bf}"])

            def p6_mm(s6):
                bf = s6 % 2
                DMA("sp", pXT[bf][:].rearrange("p (c f) -> p c f", f=128),
                    s_x2b[128 * s6:128 * (s6 + 1), :], transpose=True,
                    reads=[f"s_x2b.{s6}"], writes=[f"dT{bf}"])
                DMA("pool", pH6[bf][:, :],
                    s_x2[128 * s6:128 * (s6 + 1), :],
                    reads=[f"s_x2.{s6}"], writes=[f"dHF{bf}"])
                for ocq in range(4):
                    ps = psrot[grp[0] % 3]
                    pskey = psrotk[grp[0] % 3]
                    grp[0] += 1
                    E("pe", lambda e, ocq=ocq, ps=ps: e.matmul(
                        out=ps[:, :], lhsT=sb_ones_r[:],
                        rhs=sb_fbr[0:1, 512 * ocq:512 * (ocq + 1)], start=True, stop=False),
                        reads=["fbr", "ones_r"], writes=[pskey], inc=0)
                    for ic in range(16):
                        E("pe", lambda e, ic=ic, ocq=ocq, bf=bf, ps=ps: e.matmul(
                            out=ps[:, :], lhsT=pXT[bf][:, 128 * ic:128 * (ic + 1)],
                            rhs=pW6[ic][:, 512 * ocq:512 * (ocq + 1)],
                            start=False, stop=(ic == 15)),
                            reads=[f"dT{bf}", "wt"], writes=[],
                            inc=(1 if ic == 15 else 0))
                    E("dve", lambda e, ocq=ocq, bf=bf, ps=ps: e.tensor_tensor(
                        out=pH6[bf][:, 512 * ocq:512 * (ocq + 1)], in0=ps[:, :],
                        in1=pH6[bf][:, 512 * ocq:512 * (ocq + 1)], op=ALU.add),
                        reads=[pskey, f"dHF{bf}"], writes=[f"dHF{bf}"])

            out_toks = []

            def ln5(s5):
                _ln(s5 % 2, s5, pH, pB16, 20, "s_x2", pB16, None, kp="dH")

            def ln6(s6):
                out_toks.append(_ln(s6 % 2, s6, pH6, pB16, 26, "out", None, o_out,
                                    kp="dHF"))

            p5_mm(0)
            ln5(0)
            p5_mm(1)
            for s5 in range(NSLOT):
                p6_mm(s5)
                if s5 + 1 < NSLOT:
                    ln5(s5 + 1)
                if s5 + 2 < NSLOT:
                    p5_mm(s5 + 2)
                ln6(s5)
            return out_toks

    def _ln(bf, s, pH, pSq, col, dest, pB16, o_out=None, kp="dH"):
        c = col
        E("dve", lambda e, bf=bf: e.reduce_sum(out=sb_sm[:, c:c + 1], in_=pH[bf][:, :],
                                               axis=AX.X),
          reads=[kp + str(bf)], writes=[f"lm{bf}"])
        E("act", lambda e, bf=bf: e.activation(out=pSq[:, :], in_=pH[bf][:, :],
                                               func=AF.Square,
                                               accum_out=sb_sm[:, c + 1:c + 2]),
          reads=[kp + str(bf)], writes=["dB0", f"lq{bf}"])
        lnfence = E("act", lambda e: e.copy(out=sb_sm[0:1, 62:63], in_=sb_sm[0:1, 61:62]))
        E("dve", lambda e: e.tensor_scalar(out=sb_sm[:, c:c + 1], in0=sb_sm[:, c:c + 1],
                                           scalar1=1.0 / D, scalar2=None, op0=ALU.mult),
          reads=[f"lm{bf}"], writes=[f"lm{bf}"])
        E("dve", lambda e: e.tensor_scalar(out=sb_sm[:, c + 1:c + 2],
                                           in0=sb_sm[:, c + 1:c + 2],
                                           scalar1=1.0 / D, scalar2=EPS,
                                           op0=ALU.mult, op1=ALU.add),
          reads=[f"lq{bf}"], writes=[f"lq{bf}"], deps=[lnfence])
        E("dve", lambda e: e.scalar_tensor_tensor(out=sb_sm[:, c + 2:c + 3],
                                                  in0=sb_sm[:, c:c + 1], scalar=-1.0,
                                                  in1=sb_sm[:, c:c + 1],
                                                  op0=ALU.mult, op1=ALU.mult),
          reads=[f"lm{bf}"], writes=[f"lv{bf}"])
        E("dve", lambda e: e.tensor_tensor(out=sb_sm[:, c + 2:c + 3],
                                           in0=sb_sm[:, c + 1:c + 2],
                                           in1=sb_sm[:, c + 2:c + 3], op=ALU.add),
          reads=[f"lq{bf}", f"lv{bf}"], writes=[f"lv{bf}"])
        E("act", lambda e: e.activation(out=sb_sm[:, c + 3:c + 4], in_=sb_sm[:, c + 2:c + 3],
                                        func=AF.Ln, bias=0.0, scale=1.0),
          reads=[f"lv{bf}"], writes=[f"lln{bf}"])
        E("act", lambda e: e.activation(out=sb_sm[:, c + 4:c + 5], in_=sb_sm[:, c + 3:c + 4],
                                        func=AF.Exp, bias=0.0, scale=-0.5),
          reads=[f"lln{bf}"], writes=[f"lrs{bf}"])
        E("dve", lambda e: e.scalar_tensor_tensor(out=sb_sm[:, c + 5:c + 6],
                                                  in0=sb_sm[:, c:c + 1], scalar=-1.0,
                                                  in1=sb_sm[:, c + 4:c + 5],
                                                  op0=ALU.mult, op1=ALU.mult),
          reads=[f"lm{bf}", f"lrs{bf}"], writes=[f"lnb{bf}"])
        E("act", lambda e, bf=bf: e.activation(out=pH[bf][:, :], in_=pH[bf][:, :],
                                               func=AF.Identity, bias=sb_sm[:, c + 5:c + 6],
                                               scale=sb_sm[:, c + 4:c + 5]),
          reads=[kp + str(bf), f"lnb{bf}", f"lrs{bf}"], writes=[kp + str(bf)])
        if dest == "s_x2":
            E("dve", lambda e, bf=bf: e.tensor_copy(out=pB16[:, :], in_=pH[bf][:, :]),
              reads=[kp + str(bf)], writes=["dB0"])
            DMA("act", s_x2[128 * s:128 * (s + 1), :], pH[bf][:, :],
                reads=[kp + str(bf)], writes=[f"s_x2.{s}"])
            DMA("act", s_x2b[128 * s:128 * (s + 1), :], pB16[:, :],
                reads=["dB0"], writes=[f"s_x2b.{s}"])
            return None
        else:
            return DMA("sp", o_out[128 * s:128 * (s + 1), :], pH[bf][:, :],
                       reads=[kp + str(bf)], writes=["out"])

    out_toks = _phaseD()

    sch.finalize(block, out_toks + sch.barrier())
    es.close()
    return nc


def _stage_core(inp, core):
    b, par = core // 2, core % 2
    rows = _rows_for_par(par)
    tok = np.concatenate([np.arange(128 * i, 128 * (i + 1)) for i in rows])
    x = np.asarray(inp["L0_attn_x"][b], dtype=np.float32)
    f32 = lambda a: np.ascontiguousarray(a, dtype=np.float32)
    b16 = lambda a: np.ascontiguousarray(
        np.asarray(a, dtype=np.float32).astype(ml_dtypes.bfloat16))

    csk, ssk = _rope_tables(np.arange(T))
    csq, ssq = csk[:, tok], ssk[:, tok]

    rtm = np.zeros((HD, HD), np.float32)
    for d_ in range(64):
        rtm[d_ + 64, d_] = -1.0
        rtm[d_, d_ + 64] = 1.0

    def bcol(vec, n):
        return np.broadcast_to(np.asarray(vec, np.float32).reshape(1, -1),
                               (128, n)).copy()

    pb4 = np.asarray(inp["L0_ffn_pbias"], np.float32).reshape(128, 16, 4)
    nb4 = np.asarray(inp["L0_ffn_nbias"], np.float32).reshape(128, 16, 4)

    return dict(
        xT=b16(x.T),
        xqT=b16(x[tok].T),
        xres=f32(x[tok]),
        wq=b16(inp["L0_attn_Wq"]), wk=b16(inp["L0_attn_Wk"]),
        wv=b16(inp["L0_attn_Wv"]), wo=b16(inp["L0_attn_Wo"]),
        bq=f32(np.asarray(inp["L0_attn_bq"]).reshape(NH, 128).T),
        bk=f32(np.asarray(inp["L0_attn_bk"]).reshape(NKV, 128).T),
        bvrow=f32(np.asarray(inp["L0_attn_bv"]).reshape(1, KVD)),
        borow=f32(np.asarray(inp["L0_attn_bo"]).reshape(1, D)),
        csq=f32(csq), ssq=f32(ssq), csk=f32(csk), ssk=f32(ssk),
        rt=rtm,
        ident=np.eye(128, dtype=np.float32),
        ones_r=np.ones((1, 128), np.float32),
        mhalf_c=np.full((128, 1), -0.5, np.float32),
        half_r=np.full((1, 128), 0.5, np.float32),
        ones_c=np.ones((128, 1), np.float32),
        masks=_masks_for_par(par),
        pwT=f32(np.asarray(inp["L0_ffn_pweight"]).transpose(2, 0, 1)),
        nwT=f32(np.asarray(inp["L0_ffn_nweight"]).transpose(2, 0, 1)),
        pb=f32(pb4.transpose(0, 2, 1).reshape(128, 64)),
        nb=f32(nb4.transpose(0, 2, 1).reshape(128, 64)),
        bexps=bcol(inp["L0_ffn_bexps"], 4),
        exps=bcol(inp["L0_ffn_exps"], 4),
        maskw=bcol(inp["L0_ffn_mask_weight"], 4),
        scale=bcol(inp["L0_ffn_scale"], 1),
        bscale=bcol(inp["L0_ffn_biasscale"], 1),
    ), (b, tok)


def kernel(**inputs):
    nc = build_program()
    in_maps, metas = [], []
    for core in range(8):
        m, meta = _stage_core(inputs, core)
        in_maps.append(m)
        metas.append(meta)
    res = run_bass_kernel_spmd(nc, in_maps, core_ids=list(range(8)))
    out = np.zeros((B, T, D), dtype=np.float32)
    for core in range(8):
        b, tok = metas[core]
        out[b, tok] = res.results[core]["out"]
    return out


# version 35 (final)
# speedup vs baseline: 1.1083x; 1.1083x over previous
import numpy as np
import ml_dtypes
from contextlib import ExitStack

import concourse.bass as bass
import concourse.mybir as mybir
from concourse.bass_utils import run_bass_kernel_spmd

F32 = mybir.dt.float32
BF16 = mybir.dt.bfloat16
I32 = mybir.dt.int32
AF = mybir.ActivationFunctionType
ALU = mybir.AluOpType
AX = mybir.AxisListType

B, T, D = 4, 4096, 2048
NH, NKV, HD = 16, 4, 128
NREP = NH // NKV
KVD = NKV * HD
NSLOT = 16
C4 = float(128.0 ** 0.25)
SHIFT = 52.0
NEG = -1e9
EPS = 1e-5

ENG2BLOCK = {"sp": "sync", "act": "scalar", "dve": "vector",
             "pe": "tensor", "pool": "gpsimd"}


class Sch:

    NDMASEM = 8

    def __init__(self, nc, es):
        self.nc = nc
        self.sem = {k: es.enter_context(nc.semaphore("sem_" + k)) for k in ENG2BLOCK}
        self.cnt = {k: 0 for k in ENG2BLOCK}
        for q in ("sp", "pool", "act"):
            for i in range(Sch.NDMASEM):
                k = f"{q}~{i}"
                self.sem[k] = es.enter_context(nc.semaphore("sem_" + k.replace("~", "_")))
                self.cnt[k] = 0
        self.dmarot = {q: 0 for q in ("sp", "pool", "act")}
        self.steps = {k: [] for k in ENG2BLOCK}
        self.last_write = {}
        self.readers = {}
        self.pending = {k: [] for k in ENG2BLOCK}

    def emit(self, eng, fn, reads=(), writes=(), deps=(), inc=1, inc_sem=None):
        d = [t for t in deps if t is not None]
        if getattr(self, "_pbar", None) and self._pfirst.get(eng):
            d.extend(self._pbar)
            self._pfirst[eng] = False
        for k in reads:
            w = self.last_write.get(k)
            if w is not None:
                d.append(w)
        for k in writes:
            w = self.last_write.get(k)
            if w is not None:
                d.append(w)
            d.extend(t for t in self.readers.get(k, []) if t is not None)
        sem_key = inc_sem or eng
        self.steps[eng].append((d, fn, inc, sem_key))
        if inc:
            self.cnt[sem_key] += inc
            tok = (sem_key, self.cnt[sem_key])
            allr, allw = list(reads), list(writes)
            for (pr, pw) in self.pending[eng]:
                allr += pr
                allw += pw
            self.pending[eng] = []
            for k in allr:
                self.readers.setdefault(k, []).append(tok)
            for k in allw:
                self.last_write[k] = tok
                self.readers[k] = []
            return tok
        else:
            self.pending[eng].append((list(reads), list(writes)))
            return None

    def dma(self, eng, out, in_, reads=(), writes=(), deps=(), **kw):
        slot = self.dmarot[eng]
        self.dmarot[eng] = (slot + 1) % Sch.NDMASEM
        sub = f"{eng}~{slot}"
        d = list(deps) + [(sub, self.cnt[sub])]
        return self.emit(eng, lambda e, o=out, i=in_, k=kw: e.dma_start(out=o, in_=i, **k),
                         reads=reads, writes=writes, deps=d, inc=16, inc_sem=sub)

    def barrier(self):
        return [(k, c) for k, c in self.cnt.items() if c > 0]

    def phase(self):
        self._pbar = self.barrier()
        self._pfirst = {k: True for k in ENG2BLOCK}

    def finalize(self, block, final_deps):
        sch = self

        def emitter(name):
            def go(eng_obj):
                hwm = {}
                for deps, fn, inc, sem_key in sch.steps[name]:
                    byeng = {}
                    for (de, dc) in deps:
                        byeng[de] = max(byeng.get(de, 0), dc)
                    for de, dc in sorted(byeng.items()):
                        if dc > 0 and dc > hwm.get(de, 0):
                            eng_obj.wait_ge(sch.sem[de], dc)
                            hwm[de] = dc
                    instr = fn(eng_obj)
                    if inc:
                        instr.then_inc(sch.sem[sem_key], inc)
                if name == "sp":
                    byeng = {}
                    for (de, dc) in final_deps:
                        byeng[de] = max(byeng.get(de, 0), dc)
                    for de, dc in sorted(byeng.items()):
                        eng_obj.wait_ge(sch.sem[de], dc)
            return go

        block.sync(emitter("sp"))
        block.scalar(emitter("act"))
        block.vector(emitter("dve"))
        block.tensor(emitter("pe"))
        block.gpsimd(emitter("pool"))


def _classes():
    return [((s // 2) + 1, s % 2) for s in range(NSLOT)]


def _rows_for_par(par):
    return [4 * (v - 1) + 2 * j + par for (v, j) in _classes()]


def _rope_tables(positions):
    pos = np.asarray(positions, dtype=np.float64)
    inv = 1.0 / (1e6 ** (np.arange(0, HD, 2, dtype=np.float64) / HD))
    ang = pos[None, :] * inv[:, None]
    cs = np.concatenate([np.cos(ang)] * 2, axis=0) / C4
    ss = np.concatenate([np.sin(ang)] * 2, axis=0) / C4
    return cs.astype(np.float32), ss.astype(np.float32)


def _masks_for_par(par):
    out = np.zeros((2, 128, 512), dtype=np.float32)
    for j in range(2):
        pos = 2 * j + par
        m = np.zeros((128, 512), dtype=np.float32)
        for t in range(4):
            if t == pos:
                m[:, 128 * t:128 * (t + 1)] = np.where(
                    np.arange(128)[None, :] <= np.arange(128)[:, None], 0.0, NEG)
            elif t > pos:
                m[:, 128 * t:128 * (t + 1)] = NEG
        out[j] = m
    return out


def build_program():
    nc = bass.Bass("TRN2", target_bir_lowering=False, debug=False)

    def ap(name, shape, dt=F32, kind="ExternalInput"):
        return nc.dram_tensor(name, shape, dt, kind=kind).ap()

    i_xT = ap("xT", [D, T], BF16)
    i_xqT = ap("xqT", [D, 2048], BF16)
    i_xres = ap("xres", [2048, D])
    i_wq = ap("wq", [D, D], BF16)
    i_wk = ap("wk", [D, KVD], BF16)
    i_wv = ap("wv", [D, KVD], BF16)
    i_wo = ap("wo", [D, D], BF16)
    i_bq = ap("bq", [128, NH])
    i_bk = ap("bk", [128, NKV])
    i_bvr = ap("bvrow", [1, KVD])
    i_bor = ap("borow", [1, D])
    i_csq = ap("csq", [HD, 2048])
    i_ssq = ap("ssq", [HD, 2048])
    i_csk = ap("csk", [HD, T])
    i_ssk = ap("ssk", [HD, T])
    i_rt = ap("rt", [HD, HD])
    i_ident = ap("ident", [128, 128])
    i_ones_r = ap("ones_r", [1, 128])
    i_mhalf_c = ap("mhalf_c", [128, 1])
    i_half_r = ap("half_r", [1, 128])
    i_ones_c = ap("ones_c", [128, 1])
    i_masks = ap("masks", [2, 128, 512])
    i_pwT = ap("pwT", [4, D, D])
    i_nwT = ap("nwT", [4, D, D])
    i_pb = ap("pb", [128, 64])
    i_nb = ap("nb", [128, 64])
    i_bexps = ap("bexps", [128, 4])
    i_exps = ap("exps", [128, 4])
    i_maskw = ap("maskw", [128, 4])
    i_scale = ap("scale", [128, 1])
    i_bscale = ap("bscale", [128, 1])

    o_out = ap("out", [2048, D], F32, kind="ExternalOutput")

    s_w = nc.dram_tensor("s_w", [D, D], BF16, kind="Internal").ap()
    s_qt = nc.dram_tensor("s_qt", [NH, HD, 2048], BF16, kind="Internal").ap()
    s_at = nc.dram_tensor("s_at", [NH, HD, 2048], BF16, kind="Internal").ap()
    s_x2 = nc.dram_tensor("s_x2", [2048, D], F32, kind="Internal").ap()
    s_x2b = nc.dram_tensor("s_x2b", [2048, D], BF16, kind="Internal").ap()
    s_fb = nc.dram_tensor("s_fb", [128, 16], F32, kind="Internal").ap()

    es = ExitStack()
    S_ = es.enter_context
    sch = Sch(nc, es)
    E, DMA = sch.emit, sch.dma

    semnums = sorted(h.num for h in sch.sem.values())
    nc.gpsimd.dma_reset(range(semnums[0], semnums[-1] + 1))
    for h in sch.sem.values():
        nc.gpsimd.sem_clear(h)
    nc.all_engine_barrier()

    sb_recip = S_(nc.sbuf_tensor("sb_recip", [128, NSLOT], F32))
    sb_cstk = S_(nc.sbuf_tensor("sb_cstk", [128, NKV], F32))
    sb_id = S_(nc.sbuf_tensor("sb_id", [128, 128], BF16))
    sb_rt = S_(nc.sbuf_tensor("sb_rt", [128, 128], BF16))
    sb_ones_r = S_(nc.sbuf_tensor("sb_ones_r", [1, 128], BF16))
    sb_half_r = S_(nc.sbuf_tensor("sb_half_r", [1, 128], F32))
    sb_mhalf_c = S_(nc.sbuf_tensor("sb_mhalf_c", [128, 1], BF16))
    sb_ones_c = S_(nc.sbuf_tensor("sb_ones_c", [128, 1], BF16))
    sb_bq = S_(nc.sbuf_tensor("sb_bq", [128, NH], F32))
    sb_bk = S_(nc.sbuf_tensor("sb_bk", [128, NKV], F32))
    sb_bvr = S_(nc.sbuf_tensor("sb_bvr", [1, KVD], BF16))
    sb_fbr = S_(nc.sbuf_tensor("sb_fbr", [1, D], BF16))
    sb_em = S_(nc.sbuf_tensor("sb_em", [128, 4], F32))
    sb_emn = S_(nc.sbuf_tensor("sb_emn", [128, 4], F32))
    sb_sm = S_(nc.sbuf_tensor("sb_sm", [128, 64], F32))
    sb_smB = S_(nc.sbuf_tensor("sb_smB", [128, 192], F32))
    sb_acc = S_(nc.sbuf_tensor("sb_acc", [128, 8], F32))
    sb_i32 = S_(nc.sbuf_tensor("sb_i32", [128, 16], I32))

    ps_a = S_(nc.psum_tensor("ps_a", [128, 512], F32))
    ps_b = S_(nc.psum_tensor("ps_b", [128, 512], F32))
    ps_c = S_(nc.psum_tensor("ps_c", [128, 512], F32))
    ps_d = S_(nc.psum_tensor("ps_d", [128, 512], F32))
    ps_e = S_(nc.psum_tensor("ps_e", [128, 512], F32))
    ps_f = S_(nc.psum_tensor("ps_f", [128, 256], F32))
    ps_s = S_(nc.psum_tensor("ps_s", [128, 16], F32))
    ps_r1 = S_(nc.psum_tensor("ps_r1", [1, 512], F32))

    block = S_(nc.Block())

    DMA("pool", sb_id[:], i_ident[:, :], writes=["id"])
    DMA("pool", sb_rt[:], i_rt[:, :], writes=["rt"])
    DMA("pool", sb_ones_r[:], i_ones_r[:, :], writes=["ones_r"])
    DMA("pool", sb_half_r[:], i_half_r[:, :], writes=["half_r"])
    DMA("pool", sb_mhalf_c[:], i_mhalf_c[:, :], writes=["mhalf"])
    DMA("pool", sb_ones_c[:], i_ones_c[:, :], writes=["ones_c"])
    DMA("sp", sb_bq[:], i_bq[:, :], writes=["bq"])
    DMA("sp", sb_bk[:], i_bk[:, :], writes=["bk"])
    DMA("pool", sb_bvr[:], i_bvr[:, :], writes=["bvr"])
    DMA("sp", sb_smB[:, 0:4], i_exps[:, :], writes=["smB"])
    DMA("sp", sb_smB[:, 4:8], i_maskw[:, :], writes=["smB"])
    DMA("sp", sb_smB[:, 8:9], i_scale[:, :], writes=["smB"])
    DMA("sp", sb_smB[:, 9:10], i_bscale[:, :], writes=["smB"])
    DMA("sp", sb_smB[:, 16:80], i_pb[:, :], writes=["smB"])
    DMA("sp", sb_smB[:, 80:144], i_nb[:, :], writes=["smB"])
    DMA("sp", sb_smB[:, 144:148], i_bexps[:, :], writes=["smB"])
    es_kv = es.enter_context(ExitStack())
    sb_kt = es_kv.enter_context(nc.sbuf_tensor("sb_kt", [128, NKV * T], BF16))
    sb_v = es_kv.enter_context(nc.sbuf_tensor("sb_v", [128, 32 * KVD], BF16))
    sb_negc = es_kv.enter_context(nc.sbuf_tensor("sb_negc", [128, NSLOT * NH], F32))
    sb_masks = es_kv.enter_context(nc.sbuf_tensor("sb_masks", [128, 1024], BF16))
    DMA("pool", sb_masks[:].rearrange("p (j f) -> p j f", j=2),
        i_masks[:, :, :].rearrange("j p f -> p j f"), writes=["masks"])
    es_pr = es.enter_context(ExitStack())
    pA = [es_pr.enter_context(nc.sbuf_tensor(f"prA{i}", [128, 4096], BF16)) for i in range(2)]
    pT = [es_pr.enter_context(nc.sbuf_tensor(f"prT{i}", [128, 2048], F32)) for i in range(2)]
    pR = [es_pr.enter_context(nc.sbuf_tensor(f"prR{i}", [128, 2048], F32)) for i in range(2)]
    pK = es_pr.enter_context(nc.sbuf_tensor("prK", [128, 4096], BF16))
    sb_row = es_pr.enter_context(nc.sbuf_tensor("sb_row", [1, 512], F32))
    pSqk = es_pr.enter_context(nc.sbuf_tensor("prSqk", [128, 1024], BF16))

    def _phaseA():
        with ExitStack() as ph:
            P = ph.enter_context
            pWk = [P(nc.sbuf_tensor(f"aWk{i}", [128, 2048], BF16)) for i in range(4)]
            pWv = [P(nc.sbuf_tensor(f"aWv{i}", [128, 2048], BF16)) for i in range(4)]
            pPl = [P(nc.sbuf_tensor(f"aPl{i}", [128, 1024], F32)) for i in range(4)]
            pS = [P(nc.sbuf_tensor(f"aS{i}", [128, 2048], BF16)) for i in range(2)]
            pAcc = P(nc.sbuf_tensor("aAcc", [128, 2048], F32))
            pOut = P(nc.sbuf_tensor("aOut", [128, 2048], BF16))

            E("act", lambda e: e.activation(out=sb_smB[:, 12:16], in_=sb_smB[:, 4:8],
                                            func=AF.Sigmoid), reads=["smB"], writes=["smB"])
            E("dve", lambda e: e.tensor_tensor(out=sb_em[:], in0=sb_smB[:, 0:4],
                                               in1=sb_smB[:, 12:16], op=ALU.mult),
              reads=["smB"], writes=["em"])
            E("dve", lambda e: e.tensor_scalar(out=sb_em[:], in0=sb_em[:],
                                               scalar1=sb_smB[:, 8:9], scalar2=None,
                                               op0=ALU.mult), reads=["smB", "em"], writes=["em"])
            E("dve", lambda e: e.tensor_scalar(out=sb_emn[:], in0=sb_em[:],
                                               scalar1=-1.0, scalar2=None,
                                               op0=ALU.mult), reads=["em"], writes=["emn"])
            E("dve", lambda e: e.tensor_tensor(out=sb_smB[:, 16:80], in0=sb_smB[:, 16:80],
                                               in1=sb_smB[:, 80:144], op=ALU.subtract),
              reads=["smB"], writes=["smB"])
            for n in range(4):
                if n == 0:
                    E("dve", lambda e, n=n: e.tensor_scalar(
                        out=sb_smB[:, 160:176], in0=sb_smB[:, 16 + 16 * n:32 + 16 * n],
                        scalar1=sb_smB[:, 144 + n:145 + n], scalar2=None, op0=ALU.mult),
                        reads=["smB"], writes=["smB"])
                else:
                    E("dve", lambda e, n=n: e.scalar_tensor_tensor(
                        out=sb_smB[:, 160:176], in0=sb_smB[:, 16 + 16 * n:32 + 16 * n],
                        scalar=sb_smB[:, 144 + n:145 + n], in1=sb_smB[:, 160:176],
                        op0=ALU.mult, op1=ALU.add), reads=["smB"], writes=["smB"])
            E("dve", lambda e: e.tensor_scalar(out=sb_smB[:, 160:176], in0=sb_smB[:, 160:176],
                                               scalar1=1.0, scalar2=-1.0,
                                               op0=ALU.min, op1=ALU.max),
              reads=["smB"], writes=["smB"])
            E("dve", lambda e: e.tensor_scalar(out=sb_smB[:, 160:176], in0=sb_smB[:, 160:176],
                                               scalar1=15.0, scalar2=None, op0=ALU.mult),
              reads=["smB"], writes=["smB"])
            E("dve", lambda e: e.tensor_copy(out=sb_i32[:], in_=sb_smB[:, 160:176]),
              reads=["smB"], writes=["i32"])
            E("dve", lambda e: e.tensor_copy(out=sb_smB[:, 160:176], in_=sb_i32[:]),
              reads=["i32"], writes=["smB"])
            E("dve", lambda e: e.tensor_scalar(out=sb_smB[:, 160:176], in0=sb_smB[:, 160:176],
                                               scalar1=sb_smB[:, 9:10], scalar2=1.0 / 15.0,
                                               op0=ALU.mult, op1=ALU.mult),
              reads=["smB"], writes=["smB"])
            DMA("sp", s_fb[:, :], sb_smB[:, 160:176], reads=["smB"], writes=["s_fb"])
            DMA("pool", sb_fbr[:, :], s_fb[:, :].rearrange("p c -> (p c)")[None, :],
                reads=["s_fb"], writes=["fbr"])

            QS = ("sp", "pool", "act")
            for dc in range(16):
                DMA(QS[dc % 3], pWk[dc // 4][:, 512 * (dc % 4):512 * (dc % 4 + 1)],
                    i_wk[128 * dc:128 * (dc + 1), :], writes=["wk"])
                DMA(QS[(dc + 2) % 3], pWv[dc // 4][:, 512 * (dc % 4):512 * (dc % 4 + 1)],
                    i_wv[128 * dc:128 * (dc + 1), :], writes=["wv"])

            def p1_chunk(oc):
                for n in range(4):
                  for h in range(2):
                    st = 2 * n + h
                    pw_b, nw_b = 2 * (st % 2), 2 * (st % 2) + 1
                    cs = slice(1024 * h, 1024 * (h + 1))
                    DMA("sp", pPl[pw_b][:, :],
                        i_pwT[n, 128 * oc:128 * (oc + 1), 1024 * h:1024 * (h + 1)],
                        writes=[f"aPl{pw_b}"])
                    DMA("pool", pPl[nw_b][:, :],
                        i_nwT[n, 128 * oc:128 * (oc + 1), 1024 * h:1024 * (h + 1)],
                        writes=[f"aPl{nw_b}"])
                    E("act", lambda e, pw_b=pw_b, cs=cs: e.activation(
                        out=pS[0][:, cs], in_=pPl[pw_b][:, :], func=AF.Sigmoid),
                      reads=[f"aPl{pw_b}"], writes=[f"aS0.{h}"])
                    E("act", lambda e, nw_b=nw_b, cs=cs: e.activation(
                        out=pS[1][:, cs], in_=pPl[nw_b][:, :], func=AF.Sigmoid),
                      reads=[f"aPl{nw_b}"], writes=[f"aS1.{h}"])
                    if n == 0:
                        E("dve", lambda e, n=n, cs=cs: e.tensor_scalar(
                            out=pAcc[:, cs], in0=pS[0][:, cs], scalar1=sb_em[:, n:n + 1],
                            scalar2=None, op0=ALU.mult),
                            reads=[f"aS0.{h}", "em"], writes=[f"aAcc.{h}"])
                    else:
                        E("dve", lambda e, n=n, cs=cs: e.scalar_tensor_tensor(
                            out=pAcc[:, cs], in0=pS[0][:, cs], scalar=sb_em[:, n:n + 1],
                            in1=pAcc[:, cs], op0=ALU.mult, op1=ALU.add),
                            reads=[f"aS0.{h}", "em", f"aAcc.{h}"], writes=[f"aAcc.{h}"])
                    E("dve", lambda e, n=n, cs=cs: e.scalar_tensor_tensor(
                        out=pAcc[:, cs], in0=pS[1][:, cs], scalar=sb_emn[:, n:n + 1],
                        in1=pAcc[:, cs], op0=ALU.mult, op1=ALU.add),
                        reads=[f"aS1.{h}", "emn", f"aAcc.{h}"], writes=[f"aAcc.{h}"])
                E("dve", lambda e: e.tensor_copy(out=pOut[:, :], in_=pAcc[:, :]),
                  reads=["aAcc.0", "aAcc.1"], writes=["aOut"])
                DMA("sp", s_w[128 * oc:128 * (oc + 1), :], pOut[:, :],
                    reads=["aOut"], writes=["s_w"])

            def rope_block(tc):
                tc8 = tc % 8
                for g in range(NKV):
                    kck = f"prK.{g}.{tc % 2}"
                    awin = 1024 * (tc % 2) + 256 * g
                    bwin = 2048 + 1024 * (tc % 2) + 256 * g
                    rps = ps_c if g % 2 == 0 else ps_f
                    rpsk = "ps_c" if g % 2 == 0 else "ps_f"
                    E("pe", lambda e, rps=rps, awin=awin: e.matmul(
                        out=rps[:, 0:256], lhsT=sb_rt[:],
                        rhs=pK[:, awin:awin + 256], start=True, stop=True),
                        reads=[kck + "a", "rt"], writes=[rpsk])
                    E("act", lambda e, rps=rps, bwin=bwin: e.activation(
                        out=pK[:, bwin:bwin + 256],
                        in_=rps[:, 0:256], func=AF.Copy),
                        reads=[rpsk], writes=[kck + "b"])
                    E("dve", lambda e, g=g, tc8=tc8, awin=awin: e.tensor_tensor(
                        out=pR[0][:, 512 * (g % 2):512 * (g % 2) + 256],
                        in0=pK[:, awin:awin + 256],
                        in1=pT[0][:, 256 * tc8:256 * (tc8 + 1)], op=ALU.mult),
                        reads=[kck + "a", "prT0"], writes=[f"prR0.{g % 2}"])
                    E("pool", lambda e, g=g, tc8=tc8, bwin=bwin: e.tensor_tensor(
                        out=pR[1][:, 512 * (g % 2):512 * (g % 2) + 256],
                        in0=pK[:, bwin:bwin + 256],
                        in1=pT[1][:, 256 * tc8:256 * (tc8 + 1)], op=ALU.mult),
                        reads=[kck + "b", "prT1"], writes=[f"prR1.{g % 2}"])
                    E("dve", lambda e, g=g, tc=tc: e.tensor_tensor(
                        out=sb_kt[:, T * g + 256 * tc: T * g + 256 * (tc + 1)],
                        in0=pR[0][:, 512 * (g % 2):512 * (g % 2) + 256],
                        in1=pR[1][:, 512 * (g % 2):512 * (g % 2) + 256], op=ALU.add),
                        reads=[f"prR0.{g % 2}", f"prR1.{g % 2}"], writes=[f"kt{g}"])
                    E("dve", lambda e, g=g, tc=tc: e.tensor_tensor(
                        out=pSqk[:, 512 * (tc % 2) + 256 * (g % 2):
                                 512 * (tc % 2) + 256 * (g % 2) + 256],
                        in0=sb_kt[:, T * g + 256 * tc: T * g + 256 * (tc + 1)],
                        in1=sb_kt[:, T * g + 256 * tc: T * g + 256 * (tc + 1)],
                        op=ALU.mult),
                        reads=[f"kt{g}"], writes=[f"prSqk{g}.{tc % 2}"])

            def km_block(tc):
                for g in range(NKV):
                    E("pe", lambda e, g=g, tc=tc: e.matmul(
                        out=ps_r1[0:1, 0:256],
                        lhsT=sb_ones_c[:],
                        rhs=pSqk[:, 512 * (tc % 2) + 256 * (g % 2):
                                 512 * (tc % 2) + 256 * (g % 2) + 256],
                        start=True, stop=True),
                        reads=[f"prSqk{g}.{tc % 2}", "ones_c"], writes=["ps_r1"])
                    E("dve", lambda e, g=g, tc=tc: e.reduce_max(
                        out=sb_row[0:1, 16 * g + tc:16 * g + tc + 1],
                        in_=ps_r1[0:1, 0:256], axis=AX.X),
                        reads=["ps_r1"], writes=["row"])

            def load_chunk(tc):
                if tc >= 16:
                    return
                DMA(("pool", "sp")[tc % 2], pA[tc % 2][:].rearrange("p (c f) -> p c f", c=16),
                    i_xT[:, 256 * tc:256 * (tc + 1)].rearrange("(c p) f -> p c f", p=128),
                    writes=[f"prA{tc % 2}"])

            def p3_chunk(tc):
                half, tc8 = tc // 8, tc % 8
                load_chunk(tc + 1)
                if tc >= 2:
                    km_block(tc - 2)
                if tc >= 1:
                    rope_block(tc - 1)
                if tc8 == 0:
                    DMA("sp", pT[0][:, :], i_csk[:, 2048 * half:2048 * (half + 1)],
                        writes=["prT0"])
                    DMA("sp", pT[1][:, :], i_ssk[:, 2048 * half:2048 * (half + 1)],
                        writes=["prT1"])
                bA = pA[tc % 2]
                key = f"prA{tc % 2}"
                for g in range(NKV):
                    ps = ps_a if g % 2 == 0 else ps_b
                    pskey = "ps_a" if g % 2 == 0 else "ps_b"
                    for dc in range(16):
                        lhs = pWk[dc // 4][:, 512 * (dc % 4) + 128 * g:
                                           512 * (dc % 4) + 128 * (g + 1)]
                        E("pe", lambda e, l=lhs, b=bA, dc=dc, ps=ps: e.matmul(
                            out=ps[:, 0:256], lhsT=l, rhs=b[:, 256 * dc:256 * (dc + 1)],
                            start=(dc == 0), stop=(dc == 15)),
                            reads=["wk", key], writes=[pskey] if dc == 0 else [],
                            inc=(1 if dc == 15 else 0))
                    awin = 1024 * (tc % 2) + 256 * g
                    E("act", lambda e, g=g, ps=ps, awin=awin: e.activation(
                        out=pK[:, awin:awin + 256], in_=ps[:, 0:256],
                        func=AF.Identity, bias=sb_bk[:, g:g + 1], scale=1.0),
                        reads=[pskey, "bk"], writes=[f"prK.{g}.{tc % 2}a"])
                for rt_ in range(2):
                    ktile = 2 * tc + rt_
                    psd = ps_d if rt_ == 0 else ps_e
                    psdk = "ps_d" if rt_ == 0 else "ps_e"
                    E("pe", lambda e, psd=psd: e.matmul(out=psd[:, :], lhsT=sb_ones_r[:],
                                                        rhs=sb_bvr[:, :], start=True, stop=False),
                      reads=["ones_r", "bvr", key], writes=[psdk], inc=0)
                    for dc in range(16):
                        E("pe", lambda e, dc=dc, rt_=rt_, b=bA, psd=psd: e.matmul(
                            out=psd[:, :],
                            lhsT=b[:, 256 * dc + 128 * rt_: 256 * dc + 128 * (rt_ + 1)],
                            rhs=pWv[dc // 4][:, 512 * (dc % 4):512 * (dc % 4 + 1)],
                            start=False, stop=(dc == 15)),
                            reads=["wv", key], writes=[],
                            inc=(1 if dc == 15 else 0))
                    E("act", lambda e, ktile=ktile, psd=psd: e.activation(
                        out=sb_v[:, KVD * ktile:KVD * (ktile + 1)], in_=psd[:, :],
                        func=AF.Copy), reads=[psdk], writes=["v"])

            load_chunk(0)
            for i in range(16):
                p3_chunk(i)
                p1_chunk(i)
            km_block(14)
            rope_block(15)
            km_block(15)

            for g in range(NKV):
                E("dve", lambda e, g=g: e.reduce_max(out=sb_row[0:1, 256 + g:257 + g],
                                                     in_=sb_row[0:1, 16 * g:16 * (g + 1)],
                                                     axis=AX.X), reads=["row"], writes=["row2"])
                E("pe", lambda e, g=g: e.matmul(out=ps_s[:, g:g + 1], lhsT=sb_half_r[:],
                                                rhs=sb_row[0:1, 256 + g:257 + g],
                                                start=True, stop=True),
                  reads=["row2", "half_r"], writes=["ps_s"])
                E("dve", lambda e, g=g: e.tensor_copy(out=sb_cstk[:, g:g + 1],
                                                      in_=ps_s[:, g:g + 1]),
                  reads=["ps_s"], writes=["cstk"])
    _phaseA()
    sch.phase()

    def _phaseB():
        with ExitStack() as ph:
            P = ph.enter_context
            pQr = [P(nc.sbuf_tensor(f"bQr{i}", [128, 4096], BF16)) for i in range(2)]
            pQ = P(nc.sbuf_tensor("bQ", [128, 2048], BF16))
            pRk = P(nc.sbuf_tensor("bRk", [128, 2048], BF16))
            pSq = P(nc.sbuf_tensor("bSq", [128, 2048], BF16))
            pWq = [P(nc.sbuf_tensor(f"bWq{i}", [128, 2048], BF16)) for i in range(8)]

            DMA("sp", pT[0][:, :], i_csq[:, :], writes=["prT0"])
            DMA("sp", pT[1][:, :], i_ssq[:, :], writes=["prT1"])

            def load_wq(quarter):
                if quarter >= 4:
                    return
                o = 4 * (quarter % 2)
                for dc in range(16):
                    DMA("act",
                        pWq[o + dc // 4][:, 512 * (dc % 4):512 * (dc % 4 + 1)],
                        i_wq[128 * dc:128 * (dc + 1),
                             512 * quarter:512 * (quarter + 1)],
                        writes=[f"wq{quarter % 2}"])

            load_wq(0)
            for quarter in range(4):
                load_wq(quarter + 1)
                wqo, wqkey = 4 * (quarter % 2), f"wq{quarter % 2}"
                def load_q(tc):
                    if tc >= 8:
                        return
                    DMA(("pool", "sp")[tc % 2],
                        pA[tc % 2][:].rearrange("p (c f) -> p c f", c=16),
                        i_xqT[:, 256 * tc:256 * (tc + 1)].rearrange("(c p) f -> p c f",
                                                                   p=128),
                        writes=[f"prA{tc % 2}"])
                load_q(0)
                for tc in range(8):
                    load_q(tc + 1)
                    bA = pA[tc % 2]
                    key = f"prA{tc % 2}"
                    for hh in range(4):
                        h = 4 * quarter + hh
                        ps = ps_a if hh % 2 == 0 else ps_b
                        pskey = "ps_a" if hh % 2 == 0 else "ps_b"
                        for dc in range(16):
                            lhs = pWq[wqo + dc // 4][:, 512 * (dc % 4) + 128 * hh:
                                               512 * (dc % 4) + 128 * (hh + 1)]
                            E("pe", lambda e, l=lhs, b=bA, dc=dc, ps=ps: e.matmul(
                                out=ps[:, 0:256], lhsT=l, rhs=b[:, 256 * dc:256 * (dc + 1)],
                                start=(dc == 0), stop=(dc == 15)),
                                reads=[wqkey, key], writes=[pskey] if dc == 0 else [],
                                inc=(1 if dc == 15 else 0))
                        dst = pQr[hh // 2][:, 2048 * (hh % 2) + 256 * tc:
                                           2048 * (hh % 2) + 256 * (tc + 1)]
                        E("act", lambda e, h=h, dst=dst, ps=ps: e.activation(
                            out=dst, in_=ps[:, 0:256], func=AF.Identity,
                            bias=sb_bq[:, h:h + 1], scale=1.0),
                            reads=[pskey, "bq"], writes=[f"qraw{hh}"])
                for hh in range(4):
                    h = 4 * quarter + hh
                    qraw = pQr[hh // 2][:, 2048 * (hh % 2):2048 * (hh % 2 + 1)]
                    for cc in range(4):
                        E("pe", lambda e, qraw=qraw, cc=cc: e.matmul(
                            out=(ps_c if cc % 2 == 0 else ps_d)[:, :], lhsT=sb_rt[:],
                            rhs=qraw[:, 512 * cc:512 * (cc + 1)], start=True, stop=True),
                            reads=[f"qraw{hh}", "rt"],
                            writes=["ps_c" if cc % 2 == 0 else "ps_d"])
                        E("act", lambda e, cc=cc: e.activation(
                            out=pRk[:, 512 * cc:512 * (cc + 1)],
                            in_=(ps_c if cc % 2 == 0 else ps_d)[:, :], func=AF.Copy),
                            reads=["ps_c" if cc % 2 == 0 else "ps_d"], writes=["bRk"])
                    E("dve", lambda e, qraw=qraw: e.tensor_tensor(
                        out=pR[0][:, :], in0=qraw, in1=pT[0][:, :], op=ALU.mult),
                        reads=[f"qraw{hh}", "prT0"], writes=["prR0"])
                    E("pool", lambda e: e.tensor_tensor(
                        out=pR[1][:, :], in0=pRk[:, :], in1=pT[1][:, :], op=ALU.mult),
                        reads=["bRk", "prT1"], writes=["prR1"])
                    E("dve", lambda e: e.tensor_tensor(out=pQ[:, :], in0=pR[0][:, :],
                                                       in1=pR[1][:, :], op=ALU.add),
                      reads=["prR0", "prR1"], writes=["bQ"])
                    DMA("sp", s_qt[h, :, :], pQ[:, :], reads=["bQ"], writes=[f"s_qt{h}"])
                    E("dve", lambda e: e.tensor_tensor(out=pSq[:, :], in0=pQ[:, :],
                                                       in1=pQ[:, :], op=ALU.mult),
                      reads=["bQ"], writes=["bSq"])
                    nrot = [ps_s, ps_e, ps_f]
                    nrotk = ["ps_s", "ps_e", "ps_f"]
                    for s in range(NSLOT):
                        nps = nrot[s % 3]
                        npsk = nrotk[s % 3]
                        E("pe", lambda e, s=s, nps=nps: e.matmul(
                            out=nps[:, 0:1],
                            lhsT=pSq[:, 128 * s:128 * (s + 1)], rhs=sb_mhalf_c[:],
                            start=True, stop=True),
                            reads=["bSq", "mhalf"], writes=[npsk])
                        g = h // NREP
                        E("dve", lambda e, s=s, h=h, g=g, nps=nps: e.scalar_tensor_tensor(
                            out=sb_negc[:, NH * s + h:NH * s + h + 1],
                            in0=nps[:, 0:1], scalar=SHIFT,
                            in1=sb_cstk[:, g:g + 1], op0=ALU.add, op1=ALU.subtract),
                            reads=[npsk, "cstk"], writes=["negc"])
    _phaseB()
    es_pr.close()
    sch.phase()

    def _phaseC():
        with ExitStack() as ph:
            P = ph.enter_context
            pPT = [P(nc.sbuf_tensor(f"cPT{i}", [128, 8192], BF16, is_transpose_dest=True))
                   for i in range(4)]
            pP = [P(nc.sbuf_tensor(f"cP{i}", [128, 4096], BF16)) for i in range(6)]
            pQl = [P(nc.sbuf_tensor(f"cQ{i}", [128, 256], BF16)) for i in range(4)]
            pAT = [P(nc.sbuf_tensor(f"cAT{i}", [128, 256], BF16)) for i in range(4)]

            psrot = [ps_a, ps_b, ps_e]
            psrotk = ["ps_a", "ps_b", "ps_e"]
            units = [(v, h) for v in range(1, 9) for h in range(NH)]
            NU = len(units)
            sjc = 0

            def emit_qld(u):
                v, h = units[u]
                ub = u % 4
                DMA("pool", pQl[ub][:, :], s_qt[h, :, 256 * (v - 1):256 * v],
                    reads=[f"s_qt{h}"], writes=[f"cQ{ub}"])

            def emit_front(u):
                nonlocal sjc
                v, h = units[u]
                g = h // NREP
                ub = u % 4
                for sl in range(2):
                    s = 2 * (v - 1) + sl
                    pbuf = pP[2 * (u % 3) + sl]
                    pkey = f"cP{2 * (u % 3) + sl}"
                    for sj in range(v):
                        ps = psrot[sjc % 3]
                        pskey = psrotk[sjc % 3]
                        sjc += 1
                        E("pe", lambda e, sl=sl, sj=sj, ps=ps, g=g, ub=ub: e.matmul(
                            out=ps[:, :], lhsT=pQl[ub][:, 128 * sl:128 * (sl + 1)],
                            rhs=sb_kt[:, T * g + 512 * sj: T * g + 512 * (sj + 1)],
                            start=True, stop=(sj != v - 1)),
                            reads=[f"cQ{ub}", f"kt{g}"], writes=[pskey],
                            inc=(1 if sj != v - 1 else 0))
                        if sj == v - 1:
                            E("pe", lambda e, sl=sl, ps=ps: e.matmul(
                                out=ps[:, :], lhsT=sb_id[:],
                                rhs=sb_masks[:, 512 * sl:512 * (sl + 1)],
                                start=False, stop=True),
                                reads=["masks", "id"], writes=[])
                        E("act", lambda e, s=s, h=h, sj=sj, pbuf=pbuf, ps=ps: e.activation(
                            out=pbuf[:, 512 * sj:512 * (sj + 1)], in_=ps[:, :],
                            func=AF.Exp,
                            bias=sb_negc[:, NH * s + h:NH * s + h + 1], scale=1.0,
                            accum_out=sb_acc[:, sj:sj + 1]),
                            reads=[pskey, "negc"], writes=[pkey, f"acc{sj}"])
                    fence = E("act", lambda e: e.copy(out=sb_sm[0:1, 62:63],
                                                      in_=sb_sm[0:1, 61:62]))
                    E("dve", lambda e, sl=sl, v=v: e.reduce_sum(
                        out=sb_recip[:, sl:sl + 1], in_=sb_acc[:, 0:v], axis=AX.X),
                        reads=[f"acc{sj}" for sj in range(v)], writes=[f"rtmp{sl}"],
                        deps=[fence])
                    E("dve", lambda e, sl=sl: e.reciprocal(
                        out=sb_recip[:, sl:sl + 1], in_=sb_recip[:, sl:sl + 1]),
                        reads=[f"rtmp{sl}"], writes=[f"rtmp{sl}"])
                    E("dve", lambda e, sl=sl, v=v, pbuf=pbuf: e.tensor_scalar(
                        out=pbuf[:, 0:512 * v], in0=pbuf[:, 0:512 * v],
                        scalar1=sb_recip[:, sl:sl + 1], scalar2=None, op0=ALU.mult),
                        reads=[pkey, f"rtmp{sl}"], writes=[pkey])
                    DMA("sp",
                        pPT[ub][:].rearrange("p (j q f) -> p j q f", j=32, q=2)
                        [:, 0:4 * v, sl, :],
                        pbuf[:, 0:512 * v], transpose=True,
                        reads=[pkey], writes=[f"cPT{ub}"])

            def emit_back(u):
                v, h = units[u]
                g = h // NREP
                ub = u % 4
                pcd = ps_c if u % 2 == 0 else ps_d
                pcdk = "ps_c" if u % 2 == 0 else "ps_d"
                for j in range(4 * v):
                    E("pe", lambda e, j=j, g=g, ub=ub, pcd=pcd: e.matmul(
                        out=pcd[:, 0:256],
                        lhsT=sb_v[:, KVD * j + 128 * g: KVD * j + 128 * (g + 1)],
                        rhs=pPT[ub][:, 256 * j:256 * (j + 1)],
                        start=(j == 0), stop=(j == 4 * v - 1)),
                        reads=["v", f"cPT{ub}"],
                        writes=[pcdk] if j == 0 else [],
                        inc=(1 if j == 4 * v - 1 else 0))
                E("dve", lambda e, pcd=pcd, ub=ub: e.tensor_copy(
                    out=pAT[ub][:, :], in_=pcd[:, 0:256]),
                    reads=[pcdk], writes=[f"cAT{ub}"])
                DMA("pool", s_at[h, :, 256 * (v - 1):256 * v], pAT[ub][:, :],
                    reads=[f"cAT{ub}"], writes=[f"s_at{h}"])

            emit_qld(0)
            emit_qld(1)
            emit_qld(2)
            for u in range(NU):
                if u + 3 < NU:
                    emit_qld(u + 3)
                emit_front(u)
                if u >= 1:
                    emit_back(u - 1)
            emit_back(NU - 1)
        es_kv.close()
    _phaseC()
    sch.phase()

    def _phaseD():
        with ExitStack() as ph:
            P = ph.enter_context
            pW = [P(nc.sbuf_tensor(f"dW{i}", [128, 2048], BF16)) for i in range(16)]
            pW6 = [P(nc.sbuf_tensor(f"dW6{i}", [128, 2048], BF16)) for i in range(16)]
            pAtt = [P(nc.sbuf_tensor(f"dA{i}", [128, 2048], BF16)) for i in range(2)]
            pH = [P(nc.sbuf_tensor(f"dH{i}", [128, 2048], F32)) for i in range(2)]
            pH6 = [P(nc.sbuf_tensor(f"dG{i}", [128, 2048], F32)) for i in range(2)]
            pX = [P(nc.sbuf_tensor(f"dX{i}", [128, 2048], F32)) for i in range(2)]
            pXT = [P(nc.sbuf_tensor(f"dT{i}", [128, 2048], BF16, is_transpose_dest=True))
                   for i in range(2)]
            pB16 = P(nc.sbuf_tensor("dB0", [128, 2048], BF16))
            sb_bor = P(nc.sbuf_tensor("sb_bor", [1, D], BF16))

            DMA("pool", sb_bor[:, :], i_bor[:, :], writes=["bor"])
            for dc in range(16):
                DMA("pool", pW[dc][:, :],
                    i_wo[128 * dc:128 * (dc + 1), :], writes=["wo"])
                DMA(("sp", "act")[dc % 2], pW6[dc][:, :],
                    s_w[:, 128 * dc:128 * (dc + 1)], transpose=True,
                    reads=["s_w"], writes=["wt"])
            psrot = [ps_a, ps_b, ps_e]
            psrotk = ["ps_a", "ps_b", "ps_e"]
            grp = [0]

            def p5_mm(s5):
                bf = s5 % 2
                DMA("act", pAtt[bf][:].rearrange("p (c f) -> p c f", f=128),
                    s_at[:, :, 128 * s5:128 * (s5 + 1)].rearrange("c p f -> p c f"),
                    reads=[f"s_at{h}" for h in range(NH)], writes=[f"dA{bf}"])
                DMA("pool", pX[bf][:, :], i_xres[128 * s5:128 * (s5 + 1), :],
                    writes=[f"dX{bf}"])
                for ocq in range(4):
                    ps = psrot[grp[0] % 3]
                    pskey = psrotk[grp[0] % 3]
                    grp[0] += 1
                    E("pe", lambda e, ocq=ocq, ps=ps: e.matmul(
                        out=ps[:, :], lhsT=sb_ones_r[:],
                        rhs=sb_bor[0:1, 512 * ocq:512 * (ocq + 1)], start=True, stop=False),
                        reads=["bor", "ones_r"], writes=[pskey], inc=0)
                    for cch in range(16):
                        E("pe", lambda e, cch=cch, ocq=ocq, bf=bf, ps=ps: e.matmul(
                            out=ps[:, :], lhsT=pAtt[bf][:, 128 * cch:128 * (cch + 1)],
                            rhs=pW[cch][:, 512 * ocq:512 * (ocq + 1)],
                            start=False, stop=(cch == 15)),
                            reads=[f"dA{bf}", "wo"], writes=[],
                            inc=(1 if cch == 15 else 0))
                    E("dve", lambda e, ocq=ocq, bf=bf, ps=ps: e.tensor_tensor(
                        out=pH[bf][:, 512 * ocq:512 * (ocq + 1)], in0=ps[:, :],
                        in1=pX[bf][:, 512 * ocq:512 * (ocq + 1)], op=ALU.add),
                        reads=[pskey, f"dX{bf}"], writes=[f"dH{bf}"])

            def p6_mm(s6):
                bf = s6 % 2
                DMA("sp", pXT[bf][:].rearrange("p (c f) -> p c f", f=128),
                    s_x2b[128 * s6:128 * (s6 + 1), :], transpose=True,
                    reads=[f"s_x2b.{s6}"], writes=[f"dT{bf}"])
                DMA("pool", pH6[bf][:, :],
                    s_x2[128 * s6:128 * (s6 + 1), :],
                    reads=[f"s_x2.{s6}"], writes=[f"dHF{bf}"])
                for ocq in range(4):
                    ps = psrot[grp[0] % 3]
                    pskey = psrotk[grp[0] % 3]
                    grp[0] += 1
                    E("pe", lambda e, ocq=ocq, ps=ps: e.matmul(
                        out=ps[:, :], lhsT=sb_ones_r[:],
                        rhs=sb_fbr[0:1, 512 * ocq:512 * (ocq + 1)], start=True, stop=False),
                        reads=["fbr", "ones_r"], writes=[pskey], inc=0)
                    for ic in range(16):
                        E("pe", lambda e, ic=ic, ocq=ocq, bf=bf, ps=ps: e.matmul(
                            out=ps[:, :], lhsT=pXT[bf][:, 128 * ic:128 * (ic + 1)],
                            rhs=pW6[ic][:, 512 * ocq:512 * (ocq + 1)],
                            start=False, stop=(ic == 15)),
                            reads=[f"dT{bf}", "wt"], writes=[],
                            inc=(1 if ic == 15 else 0))
                    E("dve", lambda e, ocq=ocq, bf=bf, ps=ps: e.tensor_tensor(
                        out=pH6[bf][:, 512 * ocq:512 * (ocq + 1)], in0=ps[:, :],
                        in1=pH6[bf][:, 512 * ocq:512 * (ocq + 1)], op=ALU.add),
                        reads=[pskey, f"dHF{bf}"], writes=[f"dHF{bf}"])

            out_toks = []

            def ln5(s5):
                _ln(s5 % 2, s5, pH, pB16, 20, "s_x2", pB16, None, kp="dH")

            def ln6(s6):
                out_toks.append(_ln(s6 % 2, s6, pH6, pB16, 26, "out", None, o_out,
                                    kp="dHF"))

            p5_mm(0)
            ln5(0)
            p5_mm(1)
            for s5 in range(NSLOT):
                p6_mm(s5)
                if s5 + 1 < NSLOT:
                    ln5(s5 + 1)
                if s5 + 2 < NSLOT:
                    p5_mm(s5 + 2)
                ln6(s5)
            return out_toks

    def _ln(bf, s, pH, pSq, col, dest, pB16, o_out=None, kp="dH"):
        c = col
        E("dve", lambda e, bf=bf: e.reduce_sum(out=sb_sm[:, c:c + 1], in_=pH[bf][:, :],
                                               axis=AX.X),
          reads=[kp + str(bf)], writes=[f"lm{bf}"])
        E("act", lambda e, bf=bf: e.activation(out=pSq[:, :], in_=pH[bf][:, :],
                                               func=AF.Square,
                                               accum_out=sb_sm[:, c + 1:c + 2]),
          reads=[kp + str(bf)], writes=["dB0", f"lq{bf}"])
        lnfence = E("act", lambda e: e.copy(out=sb_sm[0:1, 62:63], in_=sb_sm[0:1, 61:62]))
        E("dve", lambda e: e.tensor_scalar(out=sb_sm[:, c:c + 1], in0=sb_sm[:, c:c + 1],
                                           scalar1=1.0 / D, scalar2=None, op0=ALU.mult),
          reads=[f"lm{bf}"], writes=[f"lm{bf}"])
        E("dve", lambda e: e.tensor_scalar(out=sb_sm[:, c + 1:c + 2],
                                           in0=sb_sm[:, c + 1:c + 2],
                                           scalar1=1.0 / D, scalar2=EPS,
                                           op0=ALU.mult, op1=ALU.add),
          reads=[f"lq{bf}"], writes=[f"lq{bf}"], deps=[lnfence])
        E("dve", lambda e: e.scalar_tensor_tensor(out=sb_sm[:, c + 2:c + 3],
                                                  in0=sb_sm[:, c:c + 1], scalar=-1.0,
                                                  in1=sb_sm[:, c:c + 1],
                                                  op0=ALU.mult, op1=ALU.mult),
          reads=[f"lm{bf}"], writes=[f"lv{bf}"])
        E("dve", lambda e: e.tensor_tensor(out=sb_sm[:, c + 2:c + 3],
                                           in0=sb_sm[:, c + 1:c + 2],
                                           in1=sb_sm[:, c + 2:c + 3], op=ALU.add),
          reads=[f"lq{bf}", f"lv{bf}"], writes=[f"lv{bf}"])
        E("act", lambda e: e.activation(out=sb_sm[:, c + 3:c + 4], in_=sb_sm[:, c + 2:c + 3],
                                        func=AF.Ln, bias=0.0, scale=1.0),
          reads=[f"lv{bf}"], writes=[f"lln{bf}"])
        E("act", lambda e: e.activation(out=sb_sm[:, c + 4:c + 5], in_=sb_sm[:, c + 3:c + 4],
                                        func=AF.Exp, bias=0.0, scale=-0.5),
          reads=[f"lln{bf}"], writes=[f"lrs{bf}"])
        E("dve", lambda e: e.scalar_tensor_tensor(out=sb_sm[:, c + 5:c + 6],
                                                  in0=sb_sm[:, c:c + 1], scalar=-1.0,
                                                  in1=sb_sm[:, c + 4:c + 5],
                                                  op0=ALU.mult, op1=ALU.mult),
          reads=[f"lm{bf}", f"lrs{bf}"], writes=[f"lnb{bf}"])
        E("act", lambda e, bf=bf: e.activation(out=pH[bf][:, :], in_=pH[bf][:, :],
                                               func=AF.Identity, bias=sb_sm[:, c + 5:c + 6],
                                               scale=sb_sm[:, c + 4:c + 5]),
          reads=[kp + str(bf), f"lnb{bf}", f"lrs{bf}"], writes=[kp + str(bf)])
        if dest == "s_x2":
            E("dve", lambda e, bf=bf: e.tensor_copy(out=pB16[:, :], in_=pH[bf][:, :]),
              reads=[kp + str(bf)], writes=["dB0"])
            DMA("act", s_x2[128 * s:128 * (s + 1), :], pH[bf][:, :],
                reads=[kp + str(bf)], writes=[f"s_x2.{s}"])
            DMA("act", s_x2b[128 * s:128 * (s + 1), :], pB16[:, :],
                reads=["dB0"], writes=[f"s_x2b.{s}"])
            return None
        else:
            return DMA("sp", o_out[128 * s:128 * (s + 1), :], pH[bf][:, :],
                       reads=[kp + str(bf)], writes=["out"])

    out_toks = _phaseD()

    sch.finalize(block, out_toks + sch.barrier())
    es.close()
    return nc


def _stage_core(inp, core):
    b, par = core // 2, core % 2
    rows = _rows_for_par(par)
    tok = np.concatenate([np.arange(128 * i, 128 * (i + 1)) for i in rows])
    x = np.asarray(inp["L0_attn_x"][b], dtype=np.float32)
    f32 = lambda a: np.ascontiguousarray(a, dtype=np.float32)
    b16 = lambda a: np.ascontiguousarray(
        np.asarray(a, dtype=np.float32).astype(ml_dtypes.bfloat16))

    csk, ssk = _rope_tables(np.arange(T))
    csq, ssq = csk[:, tok], ssk[:, tok]

    rtm = np.zeros((HD, HD), np.float32)
    for d_ in range(64):
        rtm[d_ + 64, d_] = -1.0
        rtm[d_, d_ + 64] = 1.0

    def bcol(vec, n):
        return np.broadcast_to(np.asarray(vec, np.float32).reshape(1, -1),
                               (128, n)).copy()

    pb4 = np.asarray(inp["L0_ffn_pbias"], np.float32).reshape(128, 16, 4)
    nb4 = np.asarray(inp["L0_ffn_nbias"], np.float32).reshape(128, 16, 4)

    return dict(
        xT=b16(x.T),
        xqT=b16(x[tok].T),
        xres=f32(x[tok]),
        wq=b16(inp["L0_attn_Wq"]), wk=b16(inp["L0_attn_Wk"]),
        wv=b16(inp["L0_attn_Wv"]), wo=b16(inp["L0_attn_Wo"]),
        bq=f32(np.asarray(inp["L0_attn_bq"]).reshape(NH, 128).T),
        bk=f32(np.asarray(inp["L0_attn_bk"]).reshape(NKV, 128).T),
        bvrow=f32(np.asarray(inp["L0_attn_bv"]).reshape(1, KVD)),
        borow=f32(np.asarray(inp["L0_attn_bo"]).reshape(1, D)),
        csq=f32(csq), ssq=f32(ssq), csk=f32(csk), ssk=f32(ssk),
        rt=rtm,
        ident=np.eye(128, dtype=np.float32),
        ones_r=np.ones((1, 128), np.float32),
        mhalf_c=np.full((128, 1), -0.5, np.float32),
        half_r=np.full((1, 128), 0.5, np.float32),
        ones_c=np.ones((128, 1), np.float32),
        masks=_masks_for_par(par),
        pwT=f32(np.asarray(inp["L0_ffn_pweight"]).transpose(2, 0, 1)),
        nwT=f32(np.asarray(inp["L0_ffn_nweight"]).transpose(2, 0, 1)),
        pb=f32(pb4.transpose(0, 2, 1).reshape(128, 64)),
        nb=f32(nb4.transpose(0, 2, 1).reshape(128, 64)),
        bexps=bcol(inp["L0_ffn_bexps"], 4),
        exps=bcol(inp["L0_ffn_exps"], 4),
        maskw=bcol(inp["L0_ffn_mask_weight"], 4),
        scale=bcol(inp["L0_ffn_scale"], 1),
        bscale=bcol(inp["L0_ffn_biasscale"], 1),
    ), (b, tok)


def kernel(**inputs):
    nc = build_program()
    in_maps, metas = [], []
    for core in range(8):
        m, meta = _stage_core(inputs, core)
        in_maps.append(m)
        metas.append(meta)
    res = run_bass_kernel_spmd(nc, in_maps, core_ids=list(range(8)))
    out = np.zeros((B, T, D), dtype=np.float32)
    for core in range(8):
        b, tok = metas[core]
        out[b, tok] = res.results[core]["out"]
    return out
